# version 01
# baseline (speedup 1.0000x reference)
import math
import numpy as np
from contextlib import ExitStack

import concourse.bass as bass
import concourse.mybir as mybir
from concourse.bass_utils import run_bass_kernel_spmd

F32 = mybir.dt.float32
BF16 = mybir.dt.bfloat16
ALU = mybir.AluOpType
ACT = mybir.ActivationFunctionType
AX = mybir.AxisListType.X

M32 = 12582912.0
M64 = 6755399441055744.0
EPS_DETECT = 2.0 ** -40
LN_EPS = 1e-5
SUBLN_EPS = 1e-6
QEPS = 1e-5
INV_HS = 1.0 / math.sqrt(128.0)

S, D, I3, II = 2048, 2048, 6144, 8192
NC, NI, NH = 16, 64, 16

ENGINES = ("sync", "gpsimd", "vector", "scalar", "tensor")


class Sem:
    __slots__ = ("h", "n")

    def __init__(self, h):
        self.h = h
        self.n = 0

    def inc(self, amt=1):
        self.n += amt
        return self.n


def build_program(n_phases=9, debug=()):
    nc = bass.Bass("TRN2", target_bir_lowering=False)

    x_in = nc.dram_tensor("x", [S, D], F32, kind="ExternalInput").ap()
    wqkv = nc.dram_tensor("wqkv", [D, I3], F32, kind="ExternalInput").ap()
    bqkv = nc.dram_tensor("bqkv", [I3], F32, kind="ExternalInput").ap()
    wo = nc.dram_tensor("wo", [D, D], F32, kind="ExternalInput").ap()
    bo = nc.dram_tensor("bo", [D], F32, kind="ExternalInput").ap()
    wg = nc.dram_tensor("wg", [II, D], F32, kind="ExternalInput").ap()
    wu = nc.dram_tensor("wu", [II, D], F32, kind="ExternalInput").ap()
    wd = nc.dram_tensor("wd", [D, II], F32, kind="ExternalInput").ap()
    gsub = nc.dram_tensor("gsub", [II], F32, kind="ExternalInput").ap()
    out = nc.dram_tensor("out", [S, D], F32, kind="ExternalOutput").ap()

    qkv_sp = nc.dram_tensor("qkv_sp", [I3, S], BF16).ap()
    x1_sp = nc.dram_tensor("x1_sp", [S, D], F32).ap()
    ag_spq = [nc.dram_tensor(f"ag_sp{i}", [512, II], F32).ap()
              for i in range(4)]
    wgq_sp = nc.dram_tensor("wgq_sp", [II, D], BF16).ap()
    wuq_sp = nc.dram_tensor("wuq_sp", [II, D], BF16).ap()
    wdq_sp = nc.dram_tensor("wdq_sp", [D, II], BF16).ap()
    dbg_outs = {}
    for name in debug:
        shp = {"qkv_sp": [I3, S], "x1_sp": [S, D], "ag_sp": [S, II],
               "wgq_sp": [II, D], "wuq_sp": [II, D], "wdq_sp": [D, II],
               "consts": [128, 64], "tcol": [128, 96],
               "x_fm": [128, NC * S], "dn4": [128, 4 * 2048]}[name]
        dt = BF16 if name in ("qkv_sp", "wgq_sp", "wuq_sp", "wdq_sp",
                              "x_fm") else F32
        dbg_outs[name] = nc.dram_tensor("dbg_" + name, shp, dt,
                                        kind="ExternalOutput").ap()

    stack = ExitStack()

    SBUF_BASE = (nc._sbuf_addr_for_side("left") + 31) // 32 * 32
    SBUF_TOP = 229344

    def sb(name, shape, dtype, off):
        nbytes = shape[-1] * (2 if dtype == BF16 else 4)
        assert SBUF_BASE + off + nbytes <= SBUF_TOP, (name, off + nbytes)
        return nc.alloc_sbuf_tensor_at(name, shape, dtype, offset=SBUF_BASE + off)

    c_pool = sb("c_pool", [128, 64], F32, 0)
    ones_bf = sb("ones_bf", [128, 2], BF16, 256)
    onesr_bf = sb("onesr_bf", [1, 128], BF16, 288)
    onesr_f = sb("onesr_f", [1, 128], F32, 544)
    ident_bf = sb("ident_bf", [128, 128], BF16, 1056)
    mask_ext = sb("mask_ext", [128, 1024], BF16, 1312)
    bqkv_sb = sb("bqkv_sb", [128, 48], F32, 3360)
    bo_bf = sb("bo_bf", [1, 2048], BF16, 3552)
    stat = sb("stat", [128, 64], F32, 7648)
    sqp = sb("sqp", [128, 256], F32, 7904)
    tcol = sb("tcol", [128, 96], F32, 8928)
    PE = 9312

    RA = PE
    RB = RA + 65536
    RC = RB + 65536
    RD = RC + 57344
    assert RD + 12288 <= 212832

    x_fm = sb("x_fm", [128, NC * S], BF16, RA)
    o_fm = sb("o_fm", [128, NC * S], BF16, RA)
    xq_fm = sb("xq_fm", [128, NC * S], BF16, RA)
    yq_fm = sb("yq_fm", [128, NI * 512], BF16, RA)

    xld = [sb(f"xld{i}", [128, 2048], F32, RB + i * 8192) for i in range(2)]
    wld = [sb(f"wld{i}", [128, 2048], F32, RB + 16384 + i * 8192) for i in range(2)]
    wcb = [sb(f"wcb{i}", [128, 2048], BF16, RB + 32768 + i * 4096) for i in range(2)]
    evst = [sb(f"evst{i}", [128, 2048], BF16, RB + 40960 + i * 4096) for i in range(2)]
    xcast = sb("xcast", [128, 2048], BF16, RB + 49152)
    a2q = [sb(f"a2q{i}", [128, 2048], BF16, RB + 53248 + i * 4096) for i in range(2)]
    qh = [sb(f"qh{i}", [128, 2048], BF16, RB + i * 4096) for i in range(2)]
    kh = [sb(f"kh{i}", [128, 2048], BF16, RB + 8192 + i * 4096) for i in range(2)]
    vfm = [sb(f"vfm{i}", [128, 2048], BF16, RB + 16384 + i * 4096) for i in range(2)]
    vtm = sb("vtm", [128, 2048], BF16, RB + 24576)
    p_t = [sb(f"pt{i}", [128, 512], BF16, RB + 28672 + i * 1024) for i in range(4)]
    rb_sb = sb("rb_sb", [128, 512], BF16, RB + 32768)
    lrow_bf = sb("lrow_bf", [1, 512], BF16, RB + 33792)
    q2scr = sb("q2scr", [128, 2048], BF16, RB + 34816)
    ncol = sb("ncol", [128, 8], F32, RB + 38912)
    wo_bf = sb("wo_bf", [128, NC * D], BF16, RB)
    wq_sl = sb("wq_sl", [128, 2 * 2 * NC * 512], BF16, RB)
    agld = [sb(f"agld{i}", [128, II], F32, RB + i * 32768) for i in range(2)]
    dn4 = sb("dn4", [128, 4 * 2048], F32, RB)

    mk_scr = sb("mk_scr", [128, 1024], F32, RC)
    fl_a = sb("fl_a", [128, 512], F32, RC + 4096)
    fl_b = sb("fl_b", [128, 512], F32, RC + 6144)
    idt_scr = sb("idt_scr", [128, 128], F32, RC + 8192)
    borow = sb("borow", [1, 2048], F32, RC + 8704)
    ald = [sb(f"ald{i}", [128, 2048], F32, RC + i * 8192) for i in range(2)]
    a1scr = sb("a1scr", [128, 2048], F32, RC + 16384)
    a1p = sb("a1p", [128, 208], F32, RC + 24576)
    nrow = sb("nrow", [1, 5120], F32, RC)
    xtm = [sb(f"xtm{i}", [128, 2048], F32, RC + i * 8192) for i in range(2)]
    v_sb = sb("v_sb", [128, 2048], F32, RC + 16384)
    x1_sb = sb("x1_sb", [128, 2048], F32, RC + 24576)
    u_pc = sb("u_pc", [128, 2048], F32, RC + 32768)
    xq_tm = sb("xq_tm", [128, 2048], BF16, RC + 40960)
    yq_st = sb("yq_st", [128, 2048], BF16, RC + 40960)
    g_bc = sb("g_bc", [128, II], F32, RC)
    r1_sc = [sb(f"r1_sc{i}", [128, 512], F32, RC + 32768 + i * 2048) for i in range(2)]
    tt_sc = [sb(f"tt_sc{i}", [128, 512], F32, RC + 36864 + i * 2048) for i in range(2)]
    ar_sc = [sb(f"ar_sc{i}", [128, 512], F32, RC + 40960 + i * 2048) for i in range(2)]
    ag_st = [sb(f"ag_st{i}", [128, 512], F32, RC + 45056 + i * 2048) for i in range(2)]
    dn_sb = sb("dn_sb", [128, 2048], F32, RC + 49152)
    sq_dm = sb("sq_dm", [128, 512], F32, RC + 53248)

    x1ld = sb("x1ld", [128, 2048], F32, RD)
    wdT = [sb(f"wdT{i}", [128, 1024], BF16, RD + 8192 + i * 2048) for i in range(2)]

    C_FLAG, C_FK, C_K1, C_NEGK, C_M32 = 0, 1, 2, 3, 4
    C_MWG, C_MWU, C_MWD, C_SWG, C_SWU, C_SWD = 5, 6, 7, 8, 9, 10

    def cc(i):
        return c_pool[:, i:i + 1]

    pbk = [nc.alloc_psum_tensor(f"pb{i}", [128, 512], F32) for i in range(8)]

    steps = {e: [] for e in ENGINES}
    sems = {}

    def sem(name):
        if name not in sems:
            sems[name] = Sem(stack.enter_context(nc.semaphore(name)))
        return sems[name]

    def step(e, fn, incs=()):
        def f(eng, fn=fn, incs=tuple(incs)):
            ins = fn(eng)
            for (s, amt) in incs:
                ins = ins.then_inc(s.h, amt)
        steps[e].append(f)
        for (s, amt) in incs:
            s.inc(amt)

    def wait(e, s, val=None):
        v = s.n if val is None else val
        if v <= 0:
            return
        steps[e].append(lambda eng, sm=s, vv=v: eng.wait_ge(sm.h, vv))

    def dma(q, dst, src, s):
        step(q, lambda eng, d=dst, sr=src: eng.dma_start(out=d, in_=sr), [(s, 16)])
        return s.n

    def dmat(q, dst, src, s):
        step(q, lambda eng, d=dst, sr=src: eng.dma_start_transpose(d, sr),
             [(s, 16)])
        return s.n

    ph = sem("ph")
    phase_i = [0]

    def barrier():
        phase_i[0] += 1
        t = 5 * phase_i[0]
        waits = [(s.h, s.n) for nm, s in sems.items()
                 if s is not ph and s.n > 0]
        for e in ENGINES:
            def f(eng, t=t, waits=tuple(waits)):
                for (h, v) in waits:
                    eng.wait_ge(h, v)
                eng.nop().then_inc(ph.h, 1)
                eng.wait_ge(ph.h, t)
            steps[e].append(f)
        ph.n = t

    vsf = sem("vsf")

    def vstep(fn, incs=()):
        incs = list(incs)
        assert len(incs) <= 1
        if incs:
            step("vector", fn, incs)
            s, _ = incs[0]
            wait("vector", s, s.n)
        else:
            step("vector", fn, [(vsf, 1)])
            wait("vector", vsf)

    ssf = sem("ssf")

    def sstep(fn, incs=()):
        incs = list(incs)
        assert len(incs) <= 1
        if incs:
            step("scalar", fn, incs)
            s, _ = incs[0]
            wait("scalar", s, s.n)
        else:
            step("scalar", fn, [(ssf, 1)])
            wait("scalar", ssf)

    ld0 = sem("ld0")
    cv0 = sem("cv0")
    for fcol in range(48):
        dma("sync", bqkv_sb[:, fcol:fcol + 1],
            bass.AP(bqkv.tensor, fcol * 128, [[1, 128], [1, 1]]), ld0)
    dma("sync", borow[:], bass.AP(bo.tensor, 0, [[0, 1], [1, D]]), ld0)
    step("gpsimd", lambda g: g.iota(mk_scr[:], [[-1, 1024]], base=512,
                                    channel_multiplier=1,
                                    allow_small_or_imprecise_dtypes=True),
         [(cv0, 1)])
    step("gpsimd", lambda g: g.iota(idt_scr[:], [[-1, 128]], base=0,
                                    channel_multiplier=1,
                                    allow_small_or_imprecise_dtypes=True),
         [(cv0, 1)])
    wait("vector", cv0, 2)
    vstep(lambda v: v.tensor_single_scalar(mask_ext[:], mk_scr[:], 0.0, ALU.is_le))
    vstep(lambda v: v.tensor_single_scalar(ident_bf[:], idt_scr[:], 0.0,
                                           ALU.is_equal))
    vstep(lambda v: v.memset(fl_a[:], 1.0))
    vstep(lambda v: v.tensor_single_scalar(fl_b[:], fl_a[:], EPS_DETECT, ALU.add))
    vstep(lambda v: v.tensor_single_scalar(fl_a[:], fl_b[:], -1.0, ALU.add))
    vstep(lambda v: v.tensor_single_scalar(fl_b[:], fl_a[:], 0.0, ALU.is_gt))
    vstep(lambda v: v.tensor_copy(cc(C_FLAG), fl_b[:, 0:1]))
    vstep(lambda v: v.tensor_single_scalar(cc(C_FK), cc(C_FLAG), M64, ALU.mult))
    vstep(lambda v: v.tensor_single_scalar(cc(C_K1), cc(C_FK), M32, ALU.add))
    vstep(lambda v: v.tensor_single_scalar(cc(C_NEGK), cc(C_K1), -1.0, ALU.mult))
    vstep(lambda v: v.memset(cc(C_M32), M32))
    vstep(lambda v: v.memset(stat[:, 32:33], LN_EPS))
    vstep(lambda v: v.memset(stat[:, 33:34], SUBLN_EPS))
    vstep(lambda v: v.memset(ones_bf[:], 1.0))
    vstep(lambda v: v.memset(onesr_bf[:], 1.0))
    vstep(lambda v: v.memset(onesr_f[:], 1.0))
    wait("vector", ld0, ld0.n)
    vstep(lambda v: v.tensor_copy(bo_bf[:], borow[:]))
    barrier()

    if n_phases >= 1:
        lda = sem("lda")
        cva = sem("cva")
        mma = sem("mma")
        for idx in range(192):
            w_ap = (wg, wu, wd)[idx // 64]
            t = idx % 64
            buf = ald[idx % 2]
            if idx >= 2:
                wait("sync", cva, idx - 1)
            src = bass.AP(w_ap.tensor, t * 128 * 2048, [[2048, 128], [1, 2048]])
            dma("sync", buf[:], src, lda)
            wait("scalar", lda)
            step("scalar", lambda sc, b=buf, col=idx: sc.activation(
                a1scr[:], b[:], ACT.Abs, accum_out=a1p[:, col:col + 1]),
                [(cva, 1)])
        vstep(lambda v: v.memset(stat[:, 0:1], 1.0))
        wait("tensor", cva, cva.n)
        wait("tensor", vsf)
        step("tensor", lambda t: t.matmul(pbk[0][0:1, 0:192], stat[:, 0:1],
                                          a1p[:, 0:192], start=True, stop=True),
             [(mma, 1)])
        wait("vector", mma)
        for wi in range(3):
            vstep(lambda v, wi=wi: v.reduce_sum(
                nrow[0:1, 4200 + wi:4201 + wi],
                pbk[0][0:1, wi * 64:(wi + 1) * 64], axis=AX))
        for wi in range(3):
            vstep(lambda v, wi=wi: v.tensor_single_scalar(
                nrow[0:1, 4210 + wi:4211 + wi],
                nrow[0:1, 4200 + wi:4201 + wi], 2.0 ** -24, ALU.mult))
            vstep(lambda v, wi=wi: v.tensor_single_scalar(
                nrow[0:1, 4210 + wi:4211 + wi],
                nrow[0:1, 4210 + wi:4211 + wi], QEPS, ALU.max))
            vstep(lambda v, wi=wi: v.reciprocal(
                nrow[0:1, 4213 + wi:4214 + wi], nrow[0:1, 4210 + wi:4211 + wi]))
        wait("tensor", vsf)
        step("tensor", lambda t: t.matmul(pbk[1][:, 0:6], onesr_f[:],
                                          nrow[0:1, 4210:4216],
                                          start=True, stop=True), [(mma, 1)])
        wait("scalar", mma)
        sstep(lambda sc: sc.copy(c_pool[:, C_MWG:C_MWG + 6], pbk[1][:, 0:6]))
        barrier()

    if n_phases >= 2:
        ldb = sem("ldb")
        sdn = sem("sdn")
        qdn = sem("qdn")
        spb = sem("spb")
        for idx in range(192):
            w_ap = (wg, wu, wd)[idx // 64]
            dst_sp = (wgq_sp, wuq_sp, wdq_sp)[idx // 64]
            c_sw = (C_SWG, C_SWU, C_SWD)[idx // 64]
            t = idx % 64
            buf = ald[idx % 2]
            qbuf = a2q[idx % 2]
            if idx >= 2:
                wait("sync", sdn, idx - 1)
            src = bass.AP(w_ap.tensor, t * 128 * 2048, [[2048, 128], [1, 2048]])
            dma("sync", buf[:], src, ldb)
            wait("scalar", ldb)
            if idx >= 1:
                wait("scalar", qdn, idx)
            step("scalar", lambda sc, b=buf, c=c_sw: sc.activation(
                a1scr[:], b[:], ACT.Identity, bias=cc(C_K1), scale=cc(c)),
                [(sdn, 1)])
            wait("vector", sdn)
            if idx >= 2:
                wait("vector", spb, 16 * (idx - 1))
            step("vector", lambda v, q=qbuf: v.tensor_scalar(
                q[:], a1scr[:], cc(C_NEGK), -1.0, ALU.add, ALU.max), [])
            step("vector", lambda v, q=qbuf: v.tensor_single_scalar(
                q[:], q[:], 1.0, ALU.min), [(qdn, 1)])
            wait("scalar", qdn)
            dma("scalar", bass.AP(dst_sp.tensor, t * 128 * 2048,
                                  [[2048, 128], [1, 2048]]), qbuf[:], spb)
        barrier()

    if n_phases >= 3:
        ld1 = sem("ld1")
        xc1 = sem("xc1")
        tm1 = sem("tm1")
        tv1 = sem("tv1")
        wl1 = sem("wl1")
        wc1 = sem("wc1")
        wf1 = sem("wf1")
        mm1 = sem("mm1")
        ev1 = sem("ev1")
        sp1 = sem("sp1")
        for c in range(NC):
            buf = xld[c % 2]
            if c >= 2:
                wait("sync", xc1, c - 1)
            dma("sync", buf[:], x_in[c * 128:(c + 1) * 128, :], ld1)
            wait("vector", ld1)
            if c >= 1:
                wait("vector", tm1, c * 4)
            step("vector", lambda v, b=buf: v.tensor_copy(xcast[:], b[:]),
                 [(xc1, 1)])
            wait("tensor", xc1, c + 1)
            for k in range(NC):
                grp = c * 4 + k // 4
                bank = pbk[4 + grp % 2]
                if k % 4 == 0 and grp >= 2:
                    wait("tensor", tv1, grp - 1)
                step("tensor", lambda t, k=k, bank=bank: t.matmul(
                    bank[:, (k % 4) * 128:(k % 4 + 1) * 128],
                    xcast[:, k * 128:(k + 1) * 128], ident_bf[:],
                    start=True, stop=True),
                    [(tm1, 1)] if k % 4 == 3 else [])
                if k % 4 == 3:
                    wait("vector", tm1)
                    xfm3 = x_fm.ap().rearrange("p (kk t) -> p kk t", kk=NC)
                    dst = xfm3[:, k - 3:k + 1, c * 128:(c + 1) * 128]
                    srcb = bank.ap().rearrange("p (j t) -> p j t", j=4)
                    step("vector", lambda v, d=dst, s=srcb: v.tensor_copy(d, s),
                         [(tv1, 1)])
        for f in range(48):
            buf = f % 2
            if f >= 2:
                wait("sync", wc1, f - 1)
            src = bass.AP(wqkv.tensor, f * 128,
                          [[I3, 128], [128 * I3, 16], [1, 128]])
            dma("sync", wld[buf].ap().rearrange("p (c k) -> p c k", c=16), src,
                wl1)
            if f >= 2:
                wait("vector", wf1, f - 1)
            wait("vector", wl1)
            step("vector", lambda v, b=buf: v.tensor_copy(wcb[b][:], wld[b][:]),
                 [(wc1, 1)])
            if f == 0:
                wait("tensor", tv1, 64)
            wait("tensor", wc1, f + 1)
            for q in range(4):
                bank = pbk[(f % 2) * 4 + q]
                if f >= 2:
                    wait("tensor", ev1, (f - 2) * 4 + q + 1)
                for ck in range(NC):
                    incs = []
                    if ck == NC - 1:
                        incs = [(wf1, 1)] if q == 3 else [(mm1, 1)]
                    step("tensor", lambda t, b=buf, ck=ck, q=q, bank=bank:
                         t.matmul(bank[:], wcb[b][:, ck * 128:(ck + 1) * 128],
                                  x_fm[:, ck * S + q * 512:ck * S + q * 512 + 512],
                                  start=(ck == 0), stop=(ck == NC - 1)), incs)
            ebuf = evst[f % 2]
            if f >= 2:
                wait("scalar", sp1, 16 * (f - 1))
            for q in range(4):
                bank = pbk[(f % 2) * 4 + q]
                if q < 3:
                    wait("scalar", mm1, 3 * f + q + 1)
                else:
                    wait("scalar", wf1, f + 1)
                step("scalar", lambda sc, e=ebuf, q=q, bank=bank, f=f:
                     sc.activation(e[:, q * 512:(q + 1) * 512], bank[:],
                                   ACT.Identity, bias=bqkv_sb[:, f:f + 1],
                                   scale=1.0), [(ev1, 1)])
            dma("scalar", qkv_sp[f * 128:(f + 1) * 128, :], ebuf[:], sp1)
        barrier()

    if n_phases >= 4:
        ld2 = sem("ld2")
        vm2 = sem("vm2")
        vt2 = sem("vt2")
        nq2 = sem("nq2")
        ms2 = sem("ms2")
        pds = sem("pds")
        pdv = sem("pdv")
        av2 = sem("av2")
        lr2 = sem("lr2")
        kb2 = sem("kb2")
        rb2 = sem("rb2")
        oe2 = sem("oe2")
        gchunk = [0]
        gdiag = [0]

        def head_loads(n):
            b = n % 2
            dma("sync", qh[b][:], qkv_sp[n * 128:(n + 1) * 128, :], ld2)
            dma("sync", kh[b][:], qkv_sp[D + n * 128:D + (n + 1) * 128, :], ld2)
            dma("sync", vfm[b][:], qkv_sp[2 * D + n * 128:2 * D + (n + 1) * 128, :],
                ld2)

        head_loads(0)
        for n in range(NH):
            b = n % 2
            if n + 1 < NH:
                if n >= 1:
                    wait("sync", oe2, 4 * n)
                head_loads(n + 1)
            wait("tensor", ld2, 48 * (n + 1))
            if n >= 1:
                wait("tensor", rb2, 4 * n)
            for k in range(NC):
                if k % 4 == 0 and k // 4 >= 1:
                    wait("tensor", vt2, 4 * n + k // 4)
                step("tensor", lambda t, b=b, k=k: t.matmul(
                    pbk[6][:, (k % 4) * 128:(k % 4 + 1) * 128],
                    vfm[b][:, k * 128:(k + 1) * 128], ident_bf[:],
                    start=True, stop=True), [(vm2, 1)] if k % 4 == 3 else [])
                if k % 4 == 3:
                    wait("vector", vm2)
                    step("vector", lambda v, k=k: v.tensor_copy(
                        vtm[:, (k - 3) * 128:(k + 1) * 128], pbk[6][:]),
                        [(vt2, 1)])
            wait("vector", ld2, 48 * (n + 1))
            wait("vector", kb2, kb2.n)
            for half, src_h in ((0, qh), (1, kh)):
                wait("vector", nq2)
                vstep(lambda v, b=b, s=src_h: v.tensor_mul(q2scr[:], s[b][:],
                                                           s[b][:]))
                for q in range(4):
                    wait("tensor", vsf)
                    wait("tensor", nq2)
                    step("tensor", lambda t, q=q: t.matmul(
                        pbk[7][0:1, 0:512], ones_bf[:, 0:1],
                        q2scr[:, q * 512:(q + 1) * 512], start=True, stop=True),
                        [(nq2, 1)])
                    wait("scalar", nq2)
                    step("scalar", lambda sc, q=q, half=half: sc.copy(
                        nrow[0:1, half * 2048 + q * 512:
                             half * 2048 + (q + 1) * 512],
                        pbk[7][0:1, 0:512]), [(nq2, 1)])
            wait("vector", nq2)
            vstep(lambda v: v.reduce_max(nrow[0:1, 4100:4101], nrow[0:1, 0:2048],
                                         axis=AX))
            vstep(lambda v: v.reduce_max(nrow[0:1, 4101:4102],
                                         nrow[0:1, 2048:4096], axis=AX))
            vstep(lambda v: v.tensor_mul(nrow[0:1, 4102:4103],
                                         nrow[0:1, 4100:4101],
                                         nrow[0:1, 4101:4102]))
            wait("scalar", vsf)
            sstep(lambda sc: sc.activation(nrow[0:1, 4103:4104],
                                           nrow[0:1, 4102:4103], ACT.Sqrt,
                                           scale=1.0 / 128.0))
            wait("vector", ssf)
            vstep(lambda v: v.tensor_single_scalar(nrow[0:1, 4104:4105],
                                                   nrow[0:1, 4103:4104], -1.0,
                                                   ALU.mult))
            wait("tensor", vsf)
            wait("tensor", nq2)
            step("tensor", lambda t: t.matmul(pbk[7][:, 0:1], onesr_f[:],
                                              nrow[0:1, 4104:4105],
                                              start=True, stop=True), [(nq2, 1)])
            wait("scalar", nq2)
            sstep(lambda sc: sc.copy(ncol[:, 0:1], pbk[7][:, 0:1]))

            for tb in range(4):
                nchunk = 4 * (tb + 1)
                base = gchunk[0]
                dbase = gdiag[0]

                def emit_lsum_av(j):
                    g = base + j
                    ndiag_le = dbase + max(0, j - (nchunk - 4) + 1)
                    wait("tensor", pds, g + 1)
                    if ndiag_le > 0:
                        wait("tensor", pdv, ndiag_le)
                    step("tensor", lambda t, j=j, g=g: t.matmul(
                        pbk[4][0:1, 0:512], ones_bf[:, 0:1], p_t[g % 4][:],
                        start=(j == 0), stop=(j == nchunk - 1)), [])
                    step("tensor", lambda t, j=j, g=g: t.matmul(
                        pbk[5][:], vtm[:, j * 128:(j + 1) * 128], p_t[g % 4][:],
                        start=(j == 0), stop=(j == nchunk - 1)), [(av2, 1)])

                if base > 0:
                    wait("tensor", lr2, lr2.n)
                    wait("tensor", oe2, oe2.n)
                for j in range(nchunk):
                    g = base + j
                    if g >= 4:
                        wait("tensor", pds, g - 3)
                    step("tensor", lambda t, b=b, j=j, tb=tb, g=g: t.matmul(
                        pbk[g % 4][:], kh[b][:, j * 128:(j + 1) * 128],
                        qh[b][:, tb * 512:(tb + 1) * 512],
                        start=True, stop=True), [(ms2, 1)])
                    wait("scalar", ms2, g + 1)
                    if g >= 4:
                        wait("scalar", av2, g - 3)
                    diag = j >= nchunk - 4
                    step("scalar", lambda sc, g=g: sc.activation(
                        p_t[g % 4][:], pbk[g % 4][:], ACT.Exp,
                        bias=ncol[:, 0:1], scale=INV_HS),
                        [(pds, 1)])
                    if diag:
                        a = 512 + 512 * tb - 128 * j
                        wait("vector", pds, g + 1)
                        step("vector", lambda v, g=g, a=a: v.tensor_mul(
                            p_t[g % 4][:], p_t[g % 4][:],
                            mask_ext[:, a:a + 512]), [(pdv, 1)])
                    if j >= 3:
                        emit_lsum_av(j - 3)
                for j in range(max(0, nchunk - 3), nchunk):
                    emit_lsum_av(j)
                gchunk[0] += nchunk
                gdiag[0] += 4
                wait("vector", av2, av2.n)
                vstep(lambda v: v.reciprocal(nrow[0:1, 4608:5120],
                                             pbk[4][0:1, 0:512]))
                vstep(lambda v: v.tensor_copy(lrow_bf[:], nrow[0:1, 4608:5120]),
                      [(lr2, 1)])
                wait("tensor", lr2, lr2.n)
                step("tensor", lambda t: t.matmul(pbk[6][:, 0:512], onesr_bf[:],
                                                  lrow_bf[:], start=True,
                                                  stop=True), [(kb2, 1)])
                wait("scalar", kb2, kb2.n)
                step("scalar", lambda sc: sc.copy(rb_sb[:], pbk[6][:, 0:512]),
                     [(rb2, 1)])
                wait("vector", rb2, rb2.n)
                step("vector", lambda v, n=n, tb=tb: v.tensor_mul(
                    o_fm[:, n * S + tb * 512:n * S + (tb + 1) * 512],
                    pbk[5][:], rb_sb[:]), [(oe2, 1)])
        barrier()

    if n_phases >= 5:
        ld3 = sem("ld3")
        wc3 = sem("wc3")
        mm3 = sem("mm3")
        ev3 = sem("ev3")
        st3 = sem("st3")
        sp3 = sem("sp3")
        for v16 in range(NC):
            buf = xtm[v16 % 2]
            if v16 >= 2:
                wait("sync", wc3, v16 - 1)
            dma("sync", buf[:], wo[v16 * 128:(v16 + 1) * 128, :], ld3)
            wait("vector", ld3)
            step("vector", lambda v, b=buf, v16=v16: v.tensor_copy(
                wo_bf[:, v16 * D:(v16 + 1) * D], b[:]), [(wc3, 1)])
        for tc in range(NC):
            buf = xtm[tc % 2]
            if tc == 0:
                wait("sync", wc3, NC)
            if tc >= 2:
                wait("sync", ev3, 4 * (tc - 1))
            dma("sync", buf[:], x_in[tc * 128:(tc + 1) * 128, :], ld3)
            if tc == 0:
                wait("tensor", wc3, NC)
            for q in range(4):
                bank = pbk[(tc % 2) * 4 + q]
                if tc >= 2:
                    wait("tensor", ev3, 4 * (tc - 2) + q + 1)
                for vh in range(NC):
                    step("tensor", lambda t, vh=vh, tc=tc, q=q, bank=bank:
                         t.matmul(bank[:],
                                  o_fm[:, vh * S + tc * 128:vh * S + (tc + 1) * 128],
                                  wo_bf[:, vh * D + q * 512:vh * D + q * 512 + 512],
                                  start=(vh == 0), stop=False), [])
                step("tensor", lambda t, q=q, bank=bank: t.matmul(
                    bank[:], onesr_bf[:], bo_bf[0:1, q * 512:(q + 1) * 512],
                    start=False, stop=True), [(mm3, 1)])
            wait("vector", ld3, ld3.n)
            if tc >= 1:
                wait("vector", sp3, 16 * tc)
            for q in range(4):
                bank = pbk[(tc % 2) * 4 + q]
                wait("vector", mm3, mm3.n - (3 - q))
                step("vector", lambda v, b=buf, q=q, bank=bank: v.tensor_add(
                    v_sb[:, q * 512:(q + 1) * 512], bank[:],
                    b[:, q * 512:(q + 1) * 512]), [(ev3, 1)])
            wait("scalar", ev3, ev3.n)
            c0 = (tc % 2) * 8
            sstep(lambda sc, c0=c0: sc.activation(
                x1_sb[:], v_sb[:], ACT.Identity, accum_out=stat[:, c0:c0 + 1]))
            sstep(lambda sc, c0=c0: sc.activation(
                x1_sb[:], v_sb[:], ACT.Square, accum_out=stat[:, c0 + 1:c0 + 2]))
            wait("vector", ssf)
            vstep(lambda v, c0=c0: v.tensor_single_scalar(
                stat[:, c0:c0 + 1], stat[:, c0:c0 + 1], 1.0 / D, ALU.mult))
            vstep(lambda v, c0=c0: v.tensor_single_scalar(
                stat[:, c0 + 1:c0 + 2], stat[:, c0 + 1:c0 + 2], 1.0 / D,
                ALU.mult))
            vstep(lambda v, c0=c0: v.tensor_mul(
                stat[:, c0 + 2:c0 + 3], stat[:, c0:c0 + 1], stat[:, c0:c0 + 1]))
            vstep(lambda v, c0=c0: v.tensor_sub(
                stat[:, c0 + 2:c0 + 3], stat[:, c0 + 1:c0 + 2],
                stat[:, c0 + 2:c0 + 3]))
            wait("scalar", vsf)
            sstep(lambda sc, c0=c0: sc.activation(
                stat[:, c0 + 3:c0 + 4], stat[:, c0 + 2:c0 + 3], ACT.Sqrt,
                bias=stat[:, 32:33], scale=1.0))
            wait("vector", ssf)
            vstep(lambda v, c0=c0: v.reciprocal(
                stat[:, c0 + 4:c0 + 5], stat[:, c0 + 3:c0 + 4]))
            vstep(lambda v, c0=c0: v.scalar_tensor_tensor(
                stat[:, c0 + 5:c0 + 6], stat[:, c0:c0 + 1], -1.0,
                stat[:, c0 + 4:c0 + 5], ALU.mult, ALU.mult))
            wait("scalar", vsf)
            sstep(lambda sc, c0=c0: sc.activation(
                x1_sb[:], v_sb[:], ACT.Identity, bias=stat[:, c0 + 5:c0 + 6],
                scale=stat[:, c0 + 4:c0 + 5]), [(st3, 1)])
            dma("scalar", x1_sp[tc * 128:(tc + 1) * 128, :], x1_sb[:], sp3)
        barrier()

    if n_phases >= 6:
        ldq = sem("ldq")
        uq = sem("uq")
        xqd = sem("xqd")
        tmq = sem("tmq")
        tvq = sem("tvq")
        for tc in range(NC):
            if tc >= 1:
                wait("sync", uq, tc)
            dma("sync", x1ld[:], x1_sp[tc * 128:(tc + 1) * 128, :], ldq)
            wait("vector", ldq)
            vstep(lambda v: v.reduce_max(stat[:, 40:41], x1ld[:], axis=AX,
                                         apply_absolute_value=True))
            vstep(lambda v: v.tensor_single_scalar(stat[:, 40:41],
                                                   stat[:, 40:41], QEPS,
                                                   ALU.max))
            vstep(lambda v: v.reciprocal(stat[:, 41:42], stat[:, 40:41]))
            vstep(lambda v: v.tensor_single_scalar(stat[:, 42:43],
                                                   stat[:, 41:42], 127.0,
                                                   ALU.mult))
            vstep(lambda v, tc=tc: v.reciprocal(tcol[:, tc:tc + 1],
                                                stat[:, 42:43]))
            wait("scalar", vsf)
            sstep(lambda sc: sc.activation(u_pc[:], x1ld[:], ACT.Identity,
                                           bias=cc(C_K1), scale=stat[:, 42:43]),
                  [(uq, 1)])
            wait("vector", uq)
            if tc >= 1:
                wait("vector", tmq, tc * 4)
            step("vector", lambda v: v.tensor_single_scalar(
                xq_tm[:], u_pc[:], cc(C_NEGK), ALU.add), [(xqd, 1)])
            wait("tensor", xqd, xqd.n)
            for k in range(NC):
                grp = tc * 4 + k // 4
                bank = pbk[grp % 2]
                if k % 4 == 0 and grp >= 2:
                    wait("tensor", tvq, grp - 1)
                step("tensor", lambda t, k=k, bank=bank: t.matmul(
                    bank[:, (k % 4) * 128:(k % 4 + 1) * 128],
                    xq_tm[:, k * 128:(k + 1) * 128], ident_bf[:],
                    start=True, stop=True), [(tmq, 1)] if k % 4 == 3 else [])
                if k % 4 == 3:
                    wait("vector", tmq)
                    xqf3 = xq_fm.ap().rearrange("p (kk t) -> p kk t", kk=NC)
                    dst = xqf3[:, k - 3:k + 1, tc * 128:(tc + 1) * 128]
                    srcb = bank.ap().rearrange("p (j t) -> p j t", j=4)
                    step("vector", lambda v, d=dst, s=srcb: v.tensor_copy(d, s),
                         [(tvq, 1)])
        barrier()

    if n_phases >= 7:
        ldg = sem("ldg")
        wt4 = sem("wt4")
        gf4 = sem("gf4")
        mm4 = sem("mm4")
        pf4 = sem("pf4")
        avr = sem("avr")
        agv = sem("agv")
        sq4 = sem("sq4")
        sg4 = sem("sg4")
        dma("sync", g_bc[:], bass.AP(gsub.tensor, 0, [[0, 128], [1, II]]), ldg)
        wait("vector", ldg)
        for tc in range(NC):
            vstep(lambda v, tc=tc: v.tensor_mul(tcol[:, 16 + tc:17 + tc],
                                                cc(C_MWG), tcol[:, tc:tc + 1]))
            vstep(lambda v, tc=tc: v.tensor_mul(tcol[:, 32 + tc:33 + tc],
                                                cc(C_MWU), tcol[:, tc:tc + 1]))
            vstep(lambda v, tc=tc: v.tensor_mul(stat[:, 50:51],
                                                tcol[:, 16 + tc:17 + tc],
                                                tcol[:, 16 + tc:17 + tc]))
            vstep(lambda v, tc=tc: v.tensor_mul(tcol[:, 48 + tc:49 + tc],
                                                stat[:, 50:51],
                                                tcol[:, 32 + tc:33 + tc]))

        def wsl(w, slot, hc):
            off = ((w * 2 + slot) * NC + hc) * 512
            return wq_sl[:, off:off + 512]

        for gh in range(16):
            blk, half = gh // 2, gh % 2
            slot = gh % 2
            if gh >= 2:
                wait("sync", gf4, gh - 1)
            for w_i, wsp in ((0, wgq_sp), (1, wuq_sp)):
                for hc in range(NC):
                    dmat("sync", wsl(w_i, slot, hc),
                         bass.AP(wsp.tensor,
                                 (blk * 1024 + half * 512) * D + hc * 128,
                                 [[D, 512], [1, 128]]), wt4)
            wait("tensor", wt4, wt4.n)
            for tc in range(NC):
                idx = gh * NC + tc
                pg = pbk[(tc % 2) * 2]
                pu = pbk[(tc % 2) * 2 + 1]
                if idx >= 2:
                    wait("tensor", pf4, 2 * (idx - 1))
                for hc in range(NC):
                    step("tensor", lambda t, hc=hc, tc=tc, slot=slot, pg=pg:
                         t.matmul(pg[:], xq_fm[:, hc * S + tc * 128:
                                            hc * S + (tc + 1) * 128],
                                  wsl(0, slot, hc), start=(hc == 0),
                                  stop=(hc == NC - 1)), [])
                    incs4 = []
                    if hc == NC - 1:
                        incs4 = [(gf4, 1)] if tc == NC - 1 else [(mm4, 1)]
                    step("tensor", lambda t, hc=hc, tc=tc, slot=slot, pu=pu:
                         t.matmul(pu[:], xq_fm[:, hc * S + tc * 128:
                                            hc * S + (tc + 1) * 128],
                                  wsl(1, slot, hc), start=(hc == 0),
                                  stop=(hc == NC - 1)), incs4)
                par = tc % 2
                if tc < NC - 1:
                    wait("scalar", mm4, gh * 15 + tc + 1)
                else:
                    wait("scalar", gf4, gh + 1)
                if idx >= 2:
                    wait("scalar", avr, idx - 1)
                step("scalar", lambda sc, pg=pg, par=par: sc.activation(
                    r1_sc[par][:], pg[:], ACT.Relu), [(pf4, 1)])
                wait("vector", pf4, pf4.n)
                step("vector", lambda v, pu=pu, par=par: v.tensor_mul(
                    tt_sc[par][:], r1_sc[par][:], pu[:]), [(pf4, 1)])
                if idx >= 2:
                    wait("vector", sq4, idx - 1)
                step("vector", lambda v, par=par: v.tensor_mul(
                    ar_sc[par][:], tt_sc[par][:], r1_sc[par][:]), [(avr, 1)])
                wait("scalar", avr, avr.n)
                step("scalar", lambda sc, par=par, tc=tc, gh=gh: sc.activation(
                    sq_dm[:], ar_sc[par][:], ACT.Square,
                    scale=tcol[:, 48 + tc:49 + tc],
                    accum_out=sqp[:, tc * 16 + gh:tc * 16 + gh + 1]), [(sq4, 1)])
                if idx >= 2:
                    wait("vector", sg4, 16 * (idx - 1))
                step("vector", lambda v, par=par, blk=blk, half=half:
                     v.tensor_mul(ag_st[par][:], ar_sc[par][:],
                                  g_bc[:, blk * 1024 + half * 512:
                                       blk * 1024 + half * 512 + 512]),
                     [(agv, 1)])
                wait("scalar", agv, agv.n)
                dma("scalar",
                    bass.AP(ag_spq[tc // 4].tensor,
                            (tc % 4) * 128 * II + blk * 1024 + half * 512,
                            [[II, 128], [1, 512]]), ag_st[par][:], sg4)
        barrier()

    if n_phases >= 8:
        lag = sem("lag")
        yqd = sem("yqd")
        tmy = sem("tmy")
        tvy = sem("tvy")
        wl5 = sem("wl5")
        wd5 = sem("wd5")
        mm5 = sem("mm5")
        ev5 = sem("ev5")
        lx5 = sem("lx5")
        ot5 = sem("ot5")
        gic = [0]
        for grp in range(4):
            for t4 in range(4):
                tc = grp * 4 + t4
                buf = agld[t4 % 2]
                if tc >= 2:
                    wait("sync", yqd, 4 * (tc - 1))
                if t4 == 0 and grp >= 1:
                    wait("sync", ev5, ev5.n)
                dma("sync", buf[:],
                    ag_spq[tc // 4][(tc % 4) * 128:(tc % 4 + 1) * 128, :], lag)
                wait("vector", lag)
                vstep(lambda v, b=buf: v.reduce_max(
                    stat[:, 40:41], b[:], axis=AX, apply_absolute_value=True))
                vstep(lambda v, tc=tc: v.reduce_sum(
                    stat[:, 41:42], sqp[:, tc * 16:(tc + 1) * 16], axis=AX))
                vstep(lambda v: v.tensor_single_scalar(
                    stat[:, 41:42], stat[:, 41:42], 1.0 / II, ALU.mult))
                wait("scalar", vsf)
                sstep(lambda sc: sc.activation(stat[:, 42:43], stat[:, 41:42],
                                               ACT.Sqrt, bias=stat[:, 33:34],
                                               scale=1.0))
                wait("vector", ssf)
                vstep(lambda v: v.reciprocal(stat[:, 43:44], stat[:, 42:43]))
                vstep(lambda v, tc=tc: v.tensor_mul(stat[:, 44:45],
                                                    tcol[:, 48 + tc:49 + tc],
                                                    stat[:, 43:44]))
                vstep(lambda v: v.tensor_mul(stat[:, 45:46], stat[:, 40:41],
                                             stat[:, 44:45]))
                vstep(lambda v: v.tensor_single_scalar(stat[:, 45:46],
                                                       stat[:, 45:46], QEPS,
                                                       ALU.max))
                vstep(lambda v: v.reciprocal(stat[:, 46:47], stat[:, 45:46]))
                vstep(lambda v: v.tensor_single_scalar(stat[:, 46:47],
                                                       stat[:, 46:47], 127.0,
                                                       ALU.mult))
                vstep(lambda v, tc=tc: v.reciprocal(tcol[:, 64 + tc:65 + tc],
                                                    stat[:, 46:47]))
                vstep(lambda v: v.tensor_mul(stat[:, 47:48], stat[:, 44:45],
                                             stat[:, 46:47]))
                vstep(lambda v, tc=tc: v.tensor_mul(tcol[:, 80 + tc:81 + tc],
                                                    cc(C_MWD),
                                                    tcol[:, 64 + tc:65 + tc]))
                for pc in range(4):
                    wait("scalar", vsf)
                    wait("scalar", yqd, yqd.n)
                    sstep(lambda sc, b=buf, pc=pc: sc.activation(
                        u_pc[:], b[:, pc * 2048:(pc + 1) * 2048], ACT.Identity,
                        bias=cc(C_K1), scale=stat[:, 47:48]))
                    wait("vector", ssf)
                    gpc = tc * 4 + pc
                    if gpc >= 1:
                        wait("vector", tmy, gpc * 4)
                    step("vector", lambda v: v.tensor_single_scalar(
                        yq_st[:], u_pc[:], cc(C_NEGK), ALU.add), [(yqd, 1)])
                    wait("tensor", yqd, yqd.n)
                    for k in range(NC):
                        gi = gpc * 4 + k // 4
                        bank = pbk[gi % 2]
                        if k % 4 == 0 and gi >= 2:
                            wait("tensor", tvy, gi - 1)
                        step("tensor", lambda t, k=k, bank=bank: t.matmul(
                            bank[:, (k % 4) * 128:(k % 4 + 1) * 128],
                            yq_st[:, k * 128:(k + 1) * 128], ident_bf[:],
                            start=True, stop=True),
                            [(tmy, 1)] if k % 4 == 3 else [])
                        if k % 4 == 3:
                            wait("vector", tmy)
                            yqf3 = yq_fm.ap().rearrange("p (ic t) -> p ic t",
                                                        ic=NI)
                            dst = yqf3[:, pc * 16 + k - 3:pc * 16 + k + 1,
                                       t4 * 128:(t4 + 1) * 128]
                            srcb = bank.ap().rearrange("p (j t) -> p j t", j=4)
                            step("vector", lambda v, d=dst, s=srcb:
                                 v.tensor_copy(d, s), [(tvy, 1)])
            wait("tensor", tvy, tvy.n)
            for dh in range(2):
                for ic in range(NI):
                    sl = ic % 2
                    gici = gic[0]
                    if gici >= 2:
                        wait("sync", wd5, gici - 1)
                    dmat("sync", wdT[sl][:],
                         bass.AP(wdq_sp.tensor, (dh * 1024) * II + ic * 128,
                                 [[II, 1024], [1, 128]]), wl5)
                    wait("tensor", wl5, wl5.n)
                    if ic == 0:
                        wait("tensor", ev5, ev5.n)
                    for t4 in range(4):
                        for qq in range(2):
                            bank = pbk[t4 * 2 + qq]
                            incs5 = []
                            if t4 == 3 and qq == 1:
                                incs5 = [(wd5, 1)]
                            elif ic == NI - 1 and qq == 1:
                                incs5 = [(mm5, 1)]
                            step("tensor", lambda t, sl=sl, t4=t4, qq=qq, ic=ic,
                                 bank=bank: t.matmul(
                                     bank[:],
                                     yq_fm[:, ic * 512 + t4 * 128:
                                           ic * 512 + (t4 + 1) * 128],
                                     wdT[sl][:, qq * 512:(qq + 1) * 512],
                                     start=(ic == 0), stop=(ic == NI - 1)),
                                 incs5)
                    gic[0] += 1
                for t4 in range(4):
                    tc = grp * 4 + t4
                    if t4 < 3:
                        wait("scalar", mm5, mm5.n - (2 - t4))
                    else:
                        wait("scalar", wd5, gic[0])
                    for qq in range(2):
                        bank = pbk[t4 * 2 + qq]
                        step("scalar", lambda sc, bank=bank, dh=dh, qq=qq,
                             t4=t4, tc=tc: sc.activation(
                                 dn4[:, t4 * 2048 + dh * 1024 + qq * 512:
                                     t4 * 2048 + dh * 1024 + qq * 512 + 512],
                                 bank[:], ACT.Identity,
                                 scale=tcol[:, 80 + tc:81 + tc]), [(ev5, 1)])
            for t4 in range(4):
                tc = grp * 4 + t4
                wait("sync", ot5, ot5.n)
                dma("sync", x1ld[:], x1_sp[tc * 128:(tc + 1) * 128, :], lx5)
                wait("vector", ev5, ev5.n)
                wait("vector", lx5, lx5.n)
                vstep(lambda v, t4=t4: v.tensor_add(
                    dn_sb[:], dn4[:, t4 * 2048:(t4 + 1) * 2048], x1ld[:]))
                wait("scalar", vsf)
                sstep(lambda sc: sc.activation(x1ld[:], dn_sb[:], ACT.Identity,
                                               accum_out=stat[:, 52:53]))
                sstep(lambda sc: sc.activation(x1ld[:], dn_sb[:], ACT.Square,
                                               accum_out=stat[:, 53:54]))
                wait("vector", ssf)
                vstep(lambda v: v.tensor_single_scalar(
                    stat[:, 52:53], stat[:, 52:53], 1.0 / D, ALU.mult))
                vstep(lambda v: v.tensor_single_scalar(
                    stat[:, 53:54], stat[:, 53:54], 1.0 / D, ALU.mult))
                vstep(lambda v: v.tensor_mul(stat[:, 54:55], stat[:, 52:53],
                                             stat[:, 52:53]))
                vstep(lambda v: v.tensor_sub(stat[:, 54:55], stat[:, 53:54],
                                             stat[:, 54:55]))
                wait("scalar", vsf)
                sstep(lambda sc: sc.activation(stat[:, 55:56], stat[:, 54:55],
                                               ACT.Sqrt, bias=stat[:, 32:33],
                                               scale=1.0))
                wait("vector", ssf)
                vstep(lambda v: v.reciprocal(stat[:, 56:57], stat[:, 55:56]))
                vstep(lambda v: v.scalar_tensor_tensor(
                    stat[:, 57:58], stat[:, 52:53], -1.0, stat[:, 56:57],
                    ALU.mult, ALU.mult))
                wait("scalar", vsf)
                sstep(lambda sc: sc.activation(
                    x1ld[:], dn_sb[:], ACT.Identity, bias=stat[:, 57:58],
                    scale=stat[:, 56:57]), [(ev5, 1)])
                dma("scalar", out[tc * 128:(tc + 1) * 128, :], x1ld[:], ot5)
        wait("tensor", ot5, ot5.n)
        wait("sync", ot5, ot5.n)

    if debug:
        sdbg = sem("sdbg")
        srcs = {"qkv_sp": qkv_sp, "x1_sp": x1_sp,
                "wgq_sp": wgq_sp, "wuq_sp": wuq_sp, "wdq_sp": wdq_sp}
        for name in debug:
            if name == "ag_sp":
                for gq in range(4):
                    dma("scalar", dbg_outs[name][gq * 512:(gq + 1) * 512, :],
                        ag_spq[gq][:, :], sdbg)
                continue
            if name == "consts":
                dma("scalar", dbg_outs[name], c_pool[:], sdbg)
            elif name == "x_fm":
                dma("scalar", dbg_outs[name], x_fm[:], sdbg)
            elif name == "tcol":
                dma("scalar", dbg_outs[name], tcol[:], sdbg)
            elif name == "dn4":
                dma("scalar", dbg_outs[name], dn4[:], sdbg)
            else:
                big = srcs[name]
                nr = big.shape[0]
                chunk = 512
                for r0 in range(0, nr, chunk):
                    dma("scalar", dbg_outs[name][r0:r0 + chunk, :],
                        big[r0:r0 + chunk, :], sdbg)
        wait("scalar", sdbg, sdbg.n)

    with stack:
        with nc.Block() as block:
            for e in ENGINES:
                def runner(eng, e=e):
                    for fn in steps[e]:
                        fn(eng)
                getattr(block, e)(runner)
    return nc


def kernel(L0_attn_x, L0_attn_Wqkv, L0_attn_bqkv, L0_attn_Wo, L0_attn_bo,
           L0_ffn_w_gate, L0_ffn_w_up, L0_ffn_w_down, L0_ffn_subln_weight):
    nc = build_program()
    x = np.ascontiguousarray(np.asarray(L0_attn_x, dtype=np.float32))
    shared = dict(
        wqkv=np.ascontiguousarray(np.asarray(L0_attn_Wqkv, np.float32)),
        bqkv=np.ascontiguousarray(np.asarray(L0_attn_bqkv, np.float32)),
        wo=np.ascontiguousarray(np.asarray(L0_attn_Wo, np.float32)),
        bo=np.ascontiguousarray(np.asarray(L0_attn_bo, np.float32)),
        wg=np.ascontiguousarray(np.asarray(L0_ffn_w_gate, np.float32)),
        wu=np.ascontiguousarray(np.asarray(L0_ffn_w_up, np.float32)),
        wd=np.ascontiguousarray(np.asarray(L0_ffn_w_down, np.float32)),
        gsub=np.ascontiguousarray(np.asarray(L0_ffn_subln_weight, np.float32)),
    )
    in_maps = [dict(x=np.ascontiguousarray(x[b]), **shared) for b in range(8)]
    res = run_bass_kernel_spmd(nc, in_maps, core_ids=list(range(8)))
    return np.stack([res.results[b]["out"] for b in range(8)], axis=0)


# version 02
# speedup vs baseline: 1.0014x; 1.0014x over previous
import math
import numpy as np
from contextlib import ExitStack

import concourse.bass as bass
import concourse.mybir as mybir
from concourse.bass_utils import run_bass_kernel_spmd

F32 = mybir.dt.float32
BF16 = mybir.dt.bfloat16
ALU = mybir.AluOpType
ACT = mybir.ActivationFunctionType
AX = mybir.AxisListType.X

M32 = 12582912.0
M64 = 6755399441055744.0
EPS_DETECT = 2.0 ** -40
LN_EPS = 1e-5
SUBLN_EPS = 1e-6
QEPS = 1e-5
INV_HS = 1.0 / math.sqrt(128.0)

S, D, I3, II = 2048, 2048, 6144, 8192
NC, NI, NH = 16, 64, 16

ENGINES = ("sync", "gpsimd", "vector", "scalar", "tensor")


class Sem:
    __slots__ = ("h", "n")

    def __init__(self, h):
        self.h = h
        self.n = 0

    def inc(self, amt=1):
        self.n += amt
        return self.n


def build_program(n_phases=9, debug=()):
    nc = bass.Bass("TRN2", target_bir_lowering=False,
                   detect_race_conditions=False)

    x_in = nc.dram_tensor("x", [S, D], F32, kind="ExternalInput").ap()
    wqkv = nc.dram_tensor("wqkv", [D, I3], F32, kind="ExternalInput").ap()
    bqkv = nc.dram_tensor("bqkv", [I3], F32, kind="ExternalInput").ap()
    wo = nc.dram_tensor("wo", [D, D], F32, kind="ExternalInput").ap()
    bo = nc.dram_tensor("bo", [D], F32, kind="ExternalInput").ap()
    wg = nc.dram_tensor("wg", [II, D], F32, kind="ExternalInput").ap()
    wu = nc.dram_tensor("wu", [II, D], F32, kind="ExternalInput").ap()
    wd = nc.dram_tensor("wd", [D, II], F32, kind="ExternalInput").ap()
    gsub = nc.dram_tensor("gsub", [II], F32, kind="ExternalInput").ap()
    out = nc.dram_tensor("out", [S, D], F32, kind="ExternalOutput").ap()

    qkv_sp = nc.dram_tensor("qkv_sp", [I3, S], BF16).ap()
    x1_sp = nc.dram_tensor("x1_sp", [S, D], F32).ap()
    ag_spq = [nc.dram_tensor(f"ag_sp{i}", [512, II], F32).ap()
              for i in range(4)]
    wgq_sp = nc.dram_tensor("wgq_sp", [II, D], BF16).ap()
    wuq_sp = nc.dram_tensor("wuq_sp", [II, D], BF16).ap()
    wdq_sp = nc.dram_tensor("wdq_sp", [D, II], BF16).ap()
    dbg_outs = {}
    for name in debug:
        shp = {"qkv_sp": [I3, S], "x1_sp": [S, D], "ag_sp": [S, II],
               "wgq_sp": [II, D], "wuq_sp": [II, D], "wdq_sp": [D, II],
               "consts": [128, 64], "tcol": [128, 96],
               "x_fm": [128, NC * S], "dn4": [128, 4 * 2048]}[name]
        dt = BF16 if name in ("qkv_sp", "wgq_sp", "wuq_sp", "wdq_sp",
                              "x_fm") else F32
        dbg_outs[name] = nc.dram_tensor("dbg_" + name, shp, dt,
                                        kind="ExternalOutput").ap()

    stack = ExitStack()

    SBUF_BASE = (nc._sbuf_addr_for_side("left") + 31) // 32 * 32
    SBUF_TOP = 229344

    def sb(name, shape, dtype, off):
        nbytes = shape[-1] * (2 if dtype == BF16 else 4)
        assert SBUF_BASE + off + nbytes <= SBUF_TOP, (name, off + nbytes)
        return nc.alloc_sbuf_tensor_at(name, shape, dtype, offset=SBUF_BASE + off)

    c_pool = sb("c_pool", [128, 64], F32, 0)
    ones_bf = sb("ones_bf", [128, 2], BF16, 256)
    onesr_bf = sb("onesr_bf", [1, 128], BF16, 288)
    onesr_f = sb("onesr_f", [1, 128], F32, 544)
    ident_bf = sb("ident_bf", [128, 128], BF16, 1056)
    mask_ext = sb("mask_ext", [128, 1024], BF16, 1312)
    bqkv_sb = sb("bqkv_sb", [128, 48], F32, 3360)
    bo_bf = sb("bo_bf", [1, 2048], BF16, 3552)
    stat = sb("stat", [128, 64], F32, 7648)
    sqp = sb("sqp", [128, 256], F32, 7904)
    tcol = sb("tcol", [128, 96], F32, 8928)
    PE = 9312

    RA = PE
    RB = RA + 65536
    RC = RB + 65536
    RD = RC + 57344
    assert RD + 12288 <= 212832

    x_fm = sb("x_fm", [128, NC * S], BF16, RA)
    o_fm = sb("o_fm", [128, NC * S], BF16, RA)
    xq_fm = sb("xq_fm", [128, NC * S], BF16, RA)
    yq_fm = sb("yq_fm", [128, NI * 512], BF16, RA)

    xld = [sb(f"xld{i}", [128, 2048], F32, RB + i * 8192) for i in range(2)]
    wld = [sb(f"wld{i}", [128, 2048], F32, RB + 16384 + i * 8192) for i in range(2)]
    wcb = [sb(f"wcb{i}", [128, 2048], BF16, RB + 32768 + i * 4096) for i in range(2)]
    evst = [sb(f"evst{i}", [128, 2048], BF16, RB + 40960 + i * 4096) for i in range(2)]
    xcast = sb("xcast", [128, 2048], BF16, RB + 49152)
    a2q = [sb(f"a2q{i}", [128, 2048], BF16, RB + 53248 + i * 4096) for i in range(2)]
    qh = [sb(f"qh{i}", [128, 2048], BF16, RB + i * 4096) for i in range(2)]
    kh = [sb(f"kh{i}", [128, 2048], BF16, RB + 8192 + i * 4096) for i in range(2)]
    vfm = [sb(f"vfm{i}", [128, 2048], BF16, RB + 16384 + i * 4096) for i in range(2)]
    vtm = sb("vtm", [128, 2048], BF16, RB + 24576)
    p_t = [sb(f"pt{i}", [128, 512], BF16, RB + 28672 + i * 1024) for i in range(4)]
    rb_sb = sb("rb_sb", [128, 512], BF16, RB + 32768)
    lrow_bf = sb("lrow_bf", [1, 512], BF16, RB + 33792)
    q2scr = sb("q2scr", [128, 2048], BF16, RB + 34816)
    ncol = sb("ncol", [128, 8], F32, RB + 38912)
    wo_bf = sb("wo_bf", [128, NC * D], BF16, RB)
    wq_sl = sb("wq_sl", [128, 2 * 2 * NC * 512], BF16, RB)
    agld = [sb(f"agld{i}", [128, II], F32, RB + i * 32768) for i in range(2)]
    dn4 = sb("dn4", [128, 4 * 2048], F32, RB)

    mk_scr = sb("mk_scr", [128, 1024], F32, RC)
    fl_a = sb("fl_a", [128, 512], F32, RC + 4096)
    fl_b = sb("fl_b", [128, 512], F32, RC + 6144)
    idt_scr = sb("idt_scr", [128, 128], F32, RC + 8192)
    borow = sb("borow", [1, 2048], F32, RC + 8704)
    ald = [sb(f"ald{i}", [128, 2048], F32, RC + i * 8192) for i in range(2)]
    a1scr = sb("a1scr", [128, 2048], F32, RC + 16384)
    a1p = sb("a1p", [128, 208], F32, RC + 24576)
    nrow = sb("nrow", [1, 5120], F32, RC)
    xtm = [sb(f"xtm{i}", [128, 2048], F32, RC + i * 8192) for i in range(2)]
    v_sb = sb("v_sb", [128, 2048], F32, RC + 16384)
    x1_sb = sb("x1_sb", [128, 2048], F32, RC + 24576)
    u_pc = sb("u_pc", [128, 2048], F32, RC + 32768)
    xq_tm = sb("xq_tm", [128, 2048], BF16, RC + 40960)
    yq_st = sb("yq_st", [128, 2048], BF16, RC + 40960)
    g_bc = sb("g_bc", [128, II], F32, RC)
    r1_sc = [sb(f"r1_sc{i}", [128, 512], F32, RC + 32768 + i * 2048) for i in range(2)]
    tt_sc = [sb(f"tt_sc{i}", [128, 512], F32, RC + 36864 + i * 2048) for i in range(2)]
    ar_sc = [sb(f"ar_sc{i}", [128, 512], F32, RC + 40960 + i * 2048) for i in range(2)]
    ag_st = [sb(f"ag_st{i}", [128, 512], F32, RC + 45056 + i * 2048) for i in range(2)]
    dn_sb = sb("dn_sb", [128, 2048], F32, RC + 49152)
    sq_dm = sb("sq_dm", [128, 512], F32, RC + 53248)

    x1ld = sb("x1ld", [128, 2048], F32, RD)
    wdT = [sb(f"wdT{i}", [128, 1024], BF16, RD + 8192 + i * 2048) for i in range(2)]

    C_FLAG, C_FK, C_K1, C_NEGK, C_M32 = 0, 1, 2, 3, 4
    C_MWG, C_MWU, C_MWD, C_SWG, C_SWU, C_SWD = 5, 6, 7, 8, 9, 10

    def cc(i):
        return c_pool[:, i:i + 1]

    pbk = [nc.alloc_psum_tensor(f"pb{i}", [128, 512], F32) for i in range(8)]

    steps = {e: [] for e in ENGINES}
    sems = {}

    def sem(name):
        if name not in sems:
            sems[name] = Sem(stack.enter_context(nc.semaphore(name)))
        return sems[name]

    def step(e, fn, incs=()):
        def f(eng, fn=fn, incs=tuple(incs)):
            ins = fn(eng)
            for (s, amt) in incs:
                ins = ins.then_inc(s.h, amt)
        steps[e].append(f)
        for (s, amt) in incs:
            s.inc(amt)

    def wait(e, s, val=None):
        v = s.n if val is None else val
        if v <= 0:
            return
        steps[e].append(lambda eng, sm=s, vv=v: eng.wait_ge(sm.h, vv))

    def dma(q, dst, src, s):
        step(q, lambda eng, d=dst, sr=src: eng.dma_start(out=d, in_=sr), [(s, 16)])
        return s.n

    def dmat(q, dst, src, s):
        step(q, lambda eng, d=dst, sr=src: eng.dma_start_transpose(d, sr),
             [(s, 16)])
        return s.n

    ph = sem("ph")
    phase_i = [0]

    def barrier():
        phase_i[0] += 1
        t = 5 * phase_i[0]
        waits = [(s.h, s.n) for nm, s in sems.items()
                 if s is not ph and s.n > 0]
        for e in ENGINES:
            def f(eng, t=t, waits=tuple(waits)):
                for (h, v) in waits:
                    eng.wait_ge(h, v)
                eng.nop().then_inc(ph.h, 1)
                eng.wait_ge(ph.h, t)
            steps[e].append(f)
        ph.n = t

    vsf = sem("vsf")

    def vstep(fn, incs=()):
        incs = list(incs)
        assert len(incs) <= 1
        if incs:
            step("vector", fn, incs)
            s, _ = incs[0]
            wait("vector", s, s.n)
        else:
            step("vector", fn, [(vsf, 1)])
            wait("vector", vsf)

    ssf = sem("ssf")

    def sstep(fn, incs=()):
        incs = list(incs)
        assert len(incs) <= 1
        if incs:
            step("scalar", fn, incs)
            s, _ = incs[0]
            wait("scalar", s, s.n)
        else:
            step("scalar", fn, [(ssf, 1)])
            wait("scalar", ssf)

    ld0 = sem("ld0")
    cv0 = sem("cv0")
    for fcol in range(48):
        dma("sync", bqkv_sb[:, fcol:fcol + 1],
            bass.AP(bqkv.tensor, fcol * 128, [[1, 128], [1, 1]]), ld0)
    dma("sync", borow[:], bass.AP(bo.tensor, 0, [[0, 1], [1, D]]), ld0)
    step("gpsimd", lambda g: g.iota(mk_scr[:], [[-1, 1024]], base=512,
                                    channel_multiplier=1,
                                    allow_small_or_imprecise_dtypes=True),
         [(cv0, 1)])
    step("gpsimd", lambda g: g.iota(idt_scr[:], [[-1, 128]], base=0,
                                    channel_multiplier=1,
                                    allow_small_or_imprecise_dtypes=True),
         [(cv0, 1)])
    wait("vector", cv0, 2)
    vstep(lambda v: v.tensor_single_scalar(mask_ext[:], mk_scr[:], 0.0, ALU.is_le))
    vstep(lambda v: v.tensor_single_scalar(ident_bf[:], idt_scr[:], 0.0,
                                           ALU.is_equal))
    vstep(lambda v: v.memset(fl_a[:], 1.0))
    vstep(lambda v: v.tensor_single_scalar(fl_b[:], fl_a[:], EPS_DETECT, ALU.add))
    vstep(lambda v: v.tensor_single_scalar(fl_a[:], fl_b[:], -1.0, ALU.add))
    vstep(lambda v: v.tensor_single_scalar(fl_b[:], fl_a[:], 0.0, ALU.is_gt))
    vstep(lambda v: v.tensor_copy(cc(C_FLAG), fl_b[:, 0:1]))
    vstep(lambda v: v.tensor_single_scalar(cc(C_FK), cc(C_FLAG), M64, ALU.mult))
    vstep(lambda v: v.tensor_single_scalar(cc(C_K1), cc(C_FK), M32, ALU.add))
    vstep(lambda v: v.tensor_single_scalar(cc(C_NEGK), cc(C_K1), -1.0, ALU.mult))
    vstep(lambda v: v.memset(cc(C_M32), M32))
    vstep(lambda v: v.memset(stat[:, 32:33], LN_EPS))
    vstep(lambda v: v.memset(stat[:, 33:34], SUBLN_EPS))
    vstep(lambda v: v.memset(ones_bf[:], 1.0))
    vstep(lambda v: v.memset(onesr_bf[:], 1.0))
    vstep(lambda v: v.memset(onesr_f[:], 1.0))
    wait("vector", ld0, ld0.n)
    vstep(lambda v: v.tensor_copy(bo_bf[:], borow[:]))
    barrier()

    if n_phases >= 1:
        lda = sem("lda")
        cva = sem("cva")
        mma = sem("mma")
        for idx in range(192):
            w_ap = (wg, wu, wd)[idx // 64]
            t = idx % 64
            buf = ald[idx % 2]
            if idx >= 2:
                wait("sync", cva, idx - 1)
            src = bass.AP(w_ap.tensor, t * 128 * 2048, [[2048, 128], [1, 2048]])
            dma("sync", buf[:], src, lda)
            wait("scalar", lda)
            step("scalar", lambda sc, b=buf, col=idx: sc.activation(
                a1scr[:], b[:], ACT.Abs, accum_out=a1p[:, col:col + 1]),
                [(cva, 1)])
        vstep(lambda v: v.memset(stat[:, 0:1], 1.0))
        wait("tensor", cva, cva.n)
        wait("tensor", vsf)
        step("tensor", lambda t: t.matmul(pbk[0][0:1, 0:192], stat[:, 0:1],
                                          a1p[:, 0:192], start=True, stop=True),
             [(mma, 1)])
        wait("vector", mma)
        for wi in range(3):
            vstep(lambda v, wi=wi: v.reduce_sum(
                nrow[0:1, 4200 + wi:4201 + wi],
                pbk[0][0:1, wi * 64:(wi + 1) * 64], axis=AX))
        for wi in range(3):
            vstep(lambda v, wi=wi: v.tensor_single_scalar(
                nrow[0:1, 4210 + wi:4211 + wi],
                nrow[0:1, 4200 + wi:4201 + wi], 2.0 ** -24, ALU.mult))
            vstep(lambda v, wi=wi: v.tensor_single_scalar(
                nrow[0:1, 4210 + wi:4211 + wi],
                nrow[0:1, 4210 + wi:4211 + wi], QEPS, ALU.max))
            vstep(lambda v, wi=wi: v.reciprocal(
                nrow[0:1, 4213 + wi:4214 + wi], nrow[0:1, 4210 + wi:4211 + wi]))
        wait("tensor", vsf)
        step("tensor", lambda t: t.matmul(pbk[1][:, 0:6], onesr_f[:],
                                          nrow[0:1, 4210:4216],
                                          start=True, stop=True), [(mma, 1)])
        wait("scalar", mma)
        sstep(lambda sc: sc.copy(c_pool[:, C_MWG:C_MWG + 6], pbk[1][:, 0:6]))
        barrier()

    if n_phases >= 2:
        ldb = sem("ldb")
        sdn = sem("sdn")
        qdn = sem("qdn")
        spb = sem("spb")
        for idx in range(192):
            w_ap = (wg, wu, wd)[idx // 64]
            dst_sp = (wgq_sp, wuq_sp, wdq_sp)[idx // 64]
            c_sw = (C_SWG, C_SWU, C_SWD)[idx // 64]
            t = idx % 64
            buf = ald[idx % 2]
            qbuf = a2q[idx % 2]
            if idx >= 2:
                wait("sync", sdn, idx - 1)
            src = bass.AP(w_ap.tensor, t * 128 * 2048, [[2048, 128], [1, 2048]])
            dma("sync", buf[:], src, ldb)
            wait("scalar", ldb)
            if idx >= 1:
                wait("scalar", qdn, idx)
            step("scalar", lambda sc, b=buf, c=c_sw: sc.activation(
                a1scr[:], b[:], ACT.Identity, bias=cc(C_K1), scale=cc(c)),
                [(sdn, 1)])
            wait("vector", sdn)
            if idx >= 2:
                wait("vector", spb, 16 * (idx - 1))
            step("vector", lambda v, q=qbuf: v.tensor_scalar(
                q[:], a1scr[:], cc(C_NEGK), -1.0, ALU.add, ALU.max), [])
            step("vector", lambda v, q=qbuf: v.tensor_single_scalar(
                q[:], q[:], 1.0, ALU.min), [(qdn, 1)])
            wait("scalar", qdn)
            dma("scalar", bass.AP(dst_sp.tensor, t * 128 * 2048,
                                  [[2048, 128], [1, 2048]]), qbuf[:], spb)
        barrier()

    if n_phases >= 3:
        ld1 = sem("ld1")
        xc1 = sem("xc1")
        tm1 = sem("tm1")
        tv1 = sem("tv1")
        wl1 = sem("wl1")
        wc1 = sem("wc1")
        wf1 = sem("wf1")
        mm1 = sem("mm1")
        ev1 = sem("ev1")
        sp1 = sem("sp1")
        for c in range(NC):
            buf = xld[c % 2]
            if c >= 2:
                wait("sync", xc1, c - 1)
            dma("sync", buf[:], x_in[c * 128:(c + 1) * 128, :], ld1)
            wait("vector", ld1)
            if c >= 1:
                wait("vector", tm1, c * 4)
            step("vector", lambda v, b=buf: v.tensor_copy(xcast[:], b[:]),
                 [(xc1, 1)])
            wait("tensor", xc1, c + 1)
            for k in range(NC):
                grp = c * 4 + k // 4
                bank = pbk[4 + grp % 2]
                if k % 4 == 0 and grp >= 2:
                    wait("tensor", tv1, grp - 1)
                step("tensor", lambda t, k=k, bank=bank: t.matmul(
                    bank[:, (k % 4) * 128:(k % 4 + 1) * 128],
                    xcast[:, k * 128:(k + 1) * 128], ident_bf[:],
                    start=True, stop=True),
                    [(tm1, 1)] if k % 4 == 3 else [])
                if k % 4 == 3:
                    wait("vector", tm1)
                    xfm3 = x_fm.ap().rearrange("p (kk t) -> p kk t", kk=NC)
                    dst = xfm3[:, k - 3:k + 1, c * 128:(c + 1) * 128]
                    srcb = bank.ap().rearrange("p (j t) -> p j t", j=4)
                    step("vector", lambda v, d=dst, s=srcb: v.tensor_copy(d, s),
                         [(tv1, 1)])
        for f in range(48):
            buf = f % 2
            if f >= 2:
                wait("sync", wc1, f - 1)
            src = bass.AP(wqkv.tensor, f * 128,
                          [[I3, 128], [128 * I3, 16], [1, 128]])
            dma("sync", wld[buf].ap().rearrange("p (c k) -> p c k", c=16), src,
                wl1)
            if f >= 2:
                wait("vector", wf1, f - 1)
            wait("vector", wl1)
            step("vector", lambda v, b=buf: v.tensor_copy(wcb[b][:], wld[b][:]),
                 [(wc1, 1)])
            if f == 0:
                wait("tensor", tv1, 64)
            wait("tensor", wc1, f + 1)
            for q in range(4):
                bank = pbk[(f % 2) * 4 + q]
                if f >= 2:
                    wait("tensor", ev1, (f - 2) * 4 + q + 1)
                for ck in range(NC):
                    incs = []
                    if ck == NC - 1:
                        incs = [(wf1, 1)] if q == 3 else [(mm1, 1)]
                    step("tensor", lambda t, b=buf, ck=ck, q=q, bank=bank:
                         t.matmul(bank[:], wcb[b][:, ck * 128:(ck + 1) * 128],
                                  x_fm[:, ck * S + q * 512:ck * S + q * 512 + 512],
                                  start=(ck == 0), stop=(ck == NC - 1)), incs)
            ebuf = evst[f % 2]
            if f >= 2:
                wait("scalar", sp1, 16 * (f - 1))
            for q in range(4):
                bank = pbk[(f % 2) * 4 + q]
                if q < 3:
                    wait("scalar", mm1, 3 * f + q + 1)
                else:
                    wait("scalar", wf1, f + 1)
                step("scalar", lambda sc, e=ebuf, q=q, bank=bank, f=f:
                     sc.activation(e[:, q * 512:(q + 1) * 512], bank[:],
                                   ACT.Identity, bias=bqkv_sb[:, f:f + 1],
                                   scale=1.0), [(ev1, 1)])
            dma("scalar", qkv_sp[f * 128:(f + 1) * 128, :], ebuf[:], sp1)
        barrier()

    if n_phases >= 4:
        ld2 = sem("ld2")
        vm2 = sem("vm2")
        vt2 = sem("vt2")
        nq2 = sem("nq2")
        ms2 = sem("ms2")
        pds = sem("pds")
        pdv = sem("pdv")
        av2 = sem("av2")
        lr2 = sem("lr2")
        kb2 = sem("kb2")
        rb2 = sem("rb2")
        oe2 = sem("oe2")
        gchunk = [0]
        gdiag = [0]

        def head_loads(n):
            b = n % 2
            dma("sync", qh[b][:], qkv_sp[n * 128:(n + 1) * 128, :], ld2)
            dma("sync", kh[b][:], qkv_sp[D + n * 128:D + (n + 1) * 128, :], ld2)
            dma("sync", vfm[b][:], qkv_sp[2 * D + n * 128:2 * D + (n + 1) * 128, :],
                ld2)

        head_loads(0)
        for n in range(NH):
            b = n % 2
            if n + 1 < NH:
                if n >= 1:
                    wait("sync", oe2, 4 * n)
                head_loads(n + 1)
            wait("tensor", ld2, 48 * (n + 1))
            if n >= 1:
                wait("tensor", rb2, 4 * n)
            for k in range(NC):
                if k % 4 == 0 and k // 4 >= 1:
                    wait("tensor", vt2, 4 * n + k // 4)
                step("tensor", lambda t, b=b, k=k: t.matmul(
                    pbk[6][:, (k % 4) * 128:(k % 4 + 1) * 128],
                    vfm[b][:, k * 128:(k + 1) * 128], ident_bf[:],
                    start=True, stop=True), [(vm2, 1)] if k % 4 == 3 else [])
                if k % 4 == 3:
                    wait("vector", vm2)
                    step("vector", lambda v, k=k: v.tensor_copy(
                        vtm[:, (k - 3) * 128:(k + 1) * 128], pbk[6][:]),
                        [(vt2, 1)])
            wait("vector", ld2, 48 * (n + 1))
            wait("vector", kb2, kb2.n)
            for half, src_h in ((0, qh), (1, kh)):
                wait("vector", nq2)
                vstep(lambda v, b=b, s=src_h: v.tensor_mul(q2scr[:], s[b][:],
                                                           s[b][:]))
                for q in range(4):
                    wait("tensor", vsf)
                    wait("tensor", nq2)
                    step("tensor", lambda t, q=q: t.matmul(
                        pbk[7][0:1, 0:512], ones_bf[:, 0:1],
                        q2scr[:, q * 512:(q + 1) * 512], start=True, stop=True),
                        [(nq2, 1)])
                    wait("scalar", nq2)
                    step("scalar", lambda sc, q=q, half=half: sc.copy(
                        nrow[0:1, half * 2048 + q * 512:
                             half * 2048 + (q + 1) * 512],
                        pbk[7][0:1, 0:512]), [(nq2, 1)])
            wait("vector", nq2)
            vstep(lambda v: v.reduce_max(nrow[0:1, 4100:4101], nrow[0:1, 0:2048],
                                         axis=AX))
            vstep(lambda v: v.reduce_max(nrow[0:1, 4101:4102],
                                         nrow[0:1, 2048:4096], axis=AX))
            vstep(lambda v: v.tensor_mul(nrow[0:1, 4102:4103],
                                         nrow[0:1, 4100:4101],
                                         nrow[0:1, 4101:4102]))
            wait("scalar", vsf)
            sstep(lambda sc: sc.activation(nrow[0:1, 4103:4104],
                                           nrow[0:1, 4102:4103], ACT.Sqrt,
                                           scale=1.0 / 128.0))
            wait("vector", ssf)
            vstep(lambda v: v.tensor_single_scalar(nrow[0:1, 4104:4105],
                                                   nrow[0:1, 4103:4104], -1.0,
                                                   ALU.mult))
            wait("tensor", vsf)
            wait("tensor", nq2)
            step("tensor", lambda t: t.matmul(pbk[7][:, 0:1], onesr_f[:],
                                              nrow[0:1, 4104:4105],
                                              start=True, stop=True), [(nq2, 1)])
            wait("scalar", nq2)
            sstep(lambda sc: sc.copy(ncol[:, 0:1], pbk[7][:, 0:1]))

            for tb in range(4):
                nchunk = 4 * (tb + 1)
                base = gchunk[0]
                dbase = gdiag[0]

                def emit_lsum_av(j):
                    g = base + j
                    ndiag_le = dbase + max(0, j - (nchunk - 4) + 1)
                    wait("tensor", pds, g + 1)
                    if ndiag_le > 0:
                        wait("tensor", pdv, ndiag_le)
                    step("tensor", lambda t, j=j, g=g: t.matmul(
                        pbk[4][0:1, 0:512], ones_bf[:, 0:1], p_t[g % 4][:],
                        start=(j == 0), stop=(j == nchunk - 1)), [])
                    step("tensor", lambda t, j=j, g=g: t.matmul(
                        pbk[5][:], vtm[:, j * 128:(j + 1) * 128], p_t[g % 4][:],
                        start=(j == 0), stop=(j == nchunk - 1)), [(av2, 1)])

                if base > 0:
                    wait("tensor", lr2, lr2.n)
                    wait("tensor", oe2, oe2.n)
                for j in range(nchunk):
                    g = base + j
                    if g >= 4:
                        wait("tensor", pds, g - 3)
                    step("tensor", lambda t, b=b, j=j, tb=tb, g=g: t.matmul(
                        pbk[g % 4][:], kh[b][:, j * 128:(j + 1) * 128],
                        qh[b][:, tb * 512:(tb + 1) * 512],
                        start=True, stop=True), [(ms2, 1)])
                    wait("scalar", ms2, g + 1)
                    if g >= 4:
                        wait("scalar", av2, g - 3)
                    diag = j >= nchunk - 4
                    step("scalar", lambda sc, g=g: sc.activation(
                        p_t[g % 4][:], pbk[g % 4][:], ACT.Exp,
                        bias=ncol[:, 0:1], scale=INV_HS),
                        [(pds, 1)])
                    if diag:
                        a = 512 + 512 * tb - 128 * j
                        wait("vector", pds, g + 1)
                        step("vector", lambda v, g=g, a=a: v.tensor_mul(
                            p_t[g % 4][:], p_t[g % 4][:],
                            mask_ext[:, a:a + 512]), [(pdv, 1)])
                    if j >= 3:
                        emit_lsum_av(j - 3)
                for j in range(max(0, nchunk - 3), nchunk):
                    emit_lsum_av(j)
                gchunk[0] += nchunk
                gdiag[0] += 4
                wait("vector", av2, av2.n)
                vstep(lambda v: v.reciprocal(nrow[0:1, 4608:5120],
                                             pbk[4][0:1, 0:512]))
                vstep(lambda v: v.tensor_copy(lrow_bf[:], nrow[0:1, 4608:5120]),
                      [(lr2, 1)])
                wait("tensor", lr2, lr2.n)
                step("tensor", lambda t: t.matmul(pbk[6][:, 0:512], onesr_bf[:],
                                                  lrow_bf[:], start=True,
                                                  stop=True), [(kb2, 1)])
                wait("scalar", kb2, kb2.n)
                step("scalar", lambda sc: sc.copy(rb_sb[:], pbk[6][:, 0:512]),
                     [(rb2, 1)])
                wait("vector", rb2, rb2.n)
                step("vector", lambda v, n=n, tb=tb: v.tensor_mul(
                    o_fm[:, n * S + tb * 512:n * S + (tb + 1) * 512],
                    pbk[5][:], rb_sb[:]), [(oe2, 1)])
        barrier()

    if n_phases >= 5:
        ld3 = sem("ld3")
        wc3 = sem("wc3")
        mm3 = sem("mm3")
        ev3 = sem("ev3")
        st3 = sem("st3")
        sp3 = sem("sp3")
        for v16 in range(NC):
            buf = xtm[v16 % 2]
            if v16 >= 2:
                wait("sync", wc3, v16 - 1)
            dma("sync", buf[:], wo[v16 * 128:(v16 + 1) * 128, :], ld3)
            wait("vector", ld3)
            step("vector", lambda v, b=buf, v16=v16: v.tensor_copy(
                wo_bf[:, v16 * D:(v16 + 1) * D], b[:]), [(wc3, 1)])
        for tc in range(NC):
            buf = xtm[tc % 2]
            if tc == 0:
                wait("sync", wc3, NC)
            if tc >= 2:
                wait("sync", ev3, 4 * (tc - 1))
            dma("sync", buf[:], x_in[tc * 128:(tc + 1) * 128, :], ld3)
            if tc == 0:
                wait("tensor", wc3, NC)
            for q in range(4):
                bank = pbk[(tc % 2) * 4 + q]
                if tc >= 2:
                    wait("tensor", ev3, 4 * (tc - 2) + q + 1)
                for vh in range(NC):
                    step("tensor", lambda t, vh=vh, tc=tc, q=q, bank=bank:
                         t.matmul(bank[:],
                                  o_fm[:, vh * S + tc * 128:vh * S + (tc + 1) * 128],
                                  wo_bf[:, vh * D + q * 512:vh * D + q * 512 + 512],
                                  start=(vh == 0), stop=False), [])
                step("tensor", lambda t, q=q, bank=bank: t.matmul(
                    bank[:], onesr_bf[:], bo_bf[0:1, q * 512:(q + 1) * 512],
                    start=False, stop=True), [(mm3, 1)])
            wait("vector", ld3, ld3.n)
            if tc >= 1:
                wait("vector", sp3, 16 * tc)
            for q in range(4):
                bank = pbk[(tc % 2) * 4 + q]
                wait("vector", mm3, mm3.n - (3 - q))
                step("vector", lambda v, b=buf, q=q, bank=bank: v.tensor_add(
                    v_sb[:, q * 512:(q + 1) * 512], bank[:],
                    b[:, q * 512:(q + 1) * 512]), [(ev3, 1)])
            wait("scalar", ev3, ev3.n)
            c0 = (tc % 2) * 8
            sstep(lambda sc, c0=c0: sc.activation(
                x1_sb[:], v_sb[:], ACT.Identity, accum_out=stat[:, c0:c0 + 1]))
            sstep(lambda sc, c0=c0: sc.activation(
                x1_sb[:], v_sb[:], ACT.Square, accum_out=stat[:, c0 + 1:c0 + 2]))
            wait("vector", ssf)
            vstep(lambda v, c0=c0: v.tensor_single_scalar(
                stat[:, c0:c0 + 1], stat[:, c0:c0 + 1], 1.0 / D, ALU.mult))
            vstep(lambda v, c0=c0: v.tensor_single_scalar(
                stat[:, c0 + 1:c0 + 2], stat[:, c0 + 1:c0 + 2], 1.0 / D,
                ALU.mult))
            vstep(lambda v, c0=c0: v.tensor_mul(
                stat[:, c0 + 2:c0 + 3], stat[:, c0:c0 + 1], stat[:, c0:c0 + 1]))
            vstep(lambda v, c0=c0: v.tensor_sub(
                stat[:, c0 + 2:c0 + 3], stat[:, c0 + 1:c0 + 2],
                stat[:, c0 + 2:c0 + 3]))
            wait("scalar", vsf)
            sstep(lambda sc, c0=c0: sc.activation(
                stat[:, c0 + 3:c0 + 4], stat[:, c0 + 2:c0 + 3], ACT.Sqrt,
                bias=stat[:, 32:33], scale=1.0))
            wait("vector", ssf)
            vstep(lambda v, c0=c0: v.reciprocal(
                stat[:, c0 + 4:c0 + 5], stat[:, c0 + 3:c0 + 4]))
            vstep(lambda v, c0=c0: v.scalar_tensor_tensor(
                stat[:, c0 + 5:c0 + 6], stat[:, c0:c0 + 1], -1.0,
                stat[:, c0 + 4:c0 + 5], ALU.mult, ALU.mult))
            wait("scalar", vsf)
            sstep(lambda sc, c0=c0: sc.activation(
                x1_sb[:], v_sb[:], ACT.Identity, bias=stat[:, c0 + 5:c0 + 6],
                scale=stat[:, c0 + 4:c0 + 5]), [(st3, 1)])
            dma("scalar", x1_sp[tc * 128:(tc + 1) * 128, :], x1_sb[:], sp3)
        barrier()

    if n_phases >= 6:
        ldq = sem("ldq")
        uq = sem("uq")
        xqd = sem("xqd")
        tmq = sem("tmq")
        tvq = sem("tvq")
        for tc in range(NC):
            if tc >= 1:
                wait("sync", uq, tc)
            dma("sync", x1ld[:], x1_sp[tc * 128:(tc + 1) * 128, :], ldq)
            wait("vector", ldq)
            vstep(lambda v: v.reduce_max(stat[:, 40:41], x1ld[:], axis=AX,
                                         apply_absolute_value=True))
            vstep(lambda v: v.tensor_single_scalar(stat[:, 40:41],
                                                   stat[:, 40:41], QEPS,
                                                   ALU.max))
            vstep(lambda v: v.reciprocal(stat[:, 41:42], stat[:, 40:41]))
            vstep(lambda v: v.tensor_single_scalar(stat[:, 42:43],
                                                   stat[:, 41:42], 127.0,
                                                   ALU.mult))
            vstep(lambda v, tc=tc: v.reciprocal(tcol[:, tc:tc + 1],
                                                stat[:, 42:43]))
            wait("scalar", vsf)
            sstep(lambda sc: sc.activation(u_pc[:], x1ld[:], ACT.Identity,
                                           bias=cc(C_K1), scale=stat[:, 42:43]),
                  [(uq, 1)])
            wait("vector", uq)
            if tc >= 1:
                wait("vector", tmq, tc * 4)
            step("vector", lambda v: v.tensor_single_scalar(
                xq_tm[:], u_pc[:], cc(C_NEGK), ALU.add), [(xqd, 1)])
            wait("tensor", xqd, xqd.n)
            for k in range(NC):
                grp = tc * 4 + k // 4
                bank = pbk[grp % 2]
                if k % 4 == 0 and grp >= 2:
                    wait("tensor", tvq, grp - 1)
                step("tensor", lambda t, k=k, bank=bank: t.matmul(
                    bank[:, (k % 4) * 128:(k % 4 + 1) * 128],
                    xq_tm[:, k * 128:(k + 1) * 128], ident_bf[:],
                    start=True, stop=True), [(tmq, 1)] if k % 4 == 3 else [])
                if k % 4 == 3:
                    wait("vector", tmq)
                    xqf3 = xq_fm.ap().rearrange("p (kk t) -> p kk t", kk=NC)
                    dst = xqf3[:, k - 3:k + 1, tc * 128:(tc + 1) * 128]
                    srcb = bank.ap().rearrange("p (j t) -> p j t", j=4)
                    step("vector", lambda v, d=dst, s=srcb: v.tensor_copy(d, s),
                         [(tvq, 1)])
        barrier()

    if n_phases >= 7:
        ldg = sem("ldg")
        wt4 = sem("wt4")
        gf4 = sem("gf4")
        mm4 = sem("mm4")
        pf4 = sem("pf4")
        avr = sem("avr")
        agv = sem("agv")
        sq4 = sem("sq4")
        sg4 = sem("sg4")
        dma("sync", g_bc[:], bass.AP(gsub.tensor, 0, [[0, 128], [1, II]]), ldg)
        wait("vector", ldg)
        for tc in range(NC):
            vstep(lambda v, tc=tc: v.tensor_mul(tcol[:, 16 + tc:17 + tc],
                                                cc(C_MWG), tcol[:, tc:tc + 1]))
            vstep(lambda v, tc=tc: v.tensor_mul(tcol[:, 32 + tc:33 + tc],
                                                cc(C_MWU), tcol[:, tc:tc + 1]))
            vstep(lambda v, tc=tc: v.tensor_mul(stat[:, 50:51],
                                                tcol[:, 16 + tc:17 + tc],
                                                tcol[:, 16 + tc:17 + tc]))
            vstep(lambda v, tc=tc: v.tensor_mul(tcol[:, 48 + tc:49 + tc],
                                                stat[:, 50:51],
                                                tcol[:, 32 + tc:33 + tc]))

        def wsl(w, slot, hc):
            off = ((w * 2 + slot) * NC + hc) * 512
            return wq_sl[:, off:off + 512]

        for gh in range(16):
            blk, half = gh // 2, gh % 2
            slot = gh % 2
            if gh >= 2:
                wait("sync", gf4, gh - 1)
            for w_i, wsp in ((0, wgq_sp), (1, wuq_sp)):
                for hc in range(NC):
                    dmat("sync", wsl(w_i, slot, hc),
                         bass.AP(wsp.tensor,
                                 (blk * 1024 + half * 512) * D + hc * 128,
                                 [[D, 512], [1, 128]]), wt4)
            wait("tensor", wt4, wt4.n)
            for tc in range(NC):
                idx = gh * NC + tc
                pg = pbk[(tc % 2) * 2]
                pu = pbk[(tc % 2) * 2 + 1]
                if idx >= 2:
                    wait("tensor", pf4, 2 * (idx - 1))
                for hc in range(NC):
                    step("tensor", lambda t, hc=hc, tc=tc, slot=slot, pg=pg:
                         t.matmul(pg[:], xq_fm[:, hc * S + tc * 128:
                                            hc * S + (tc + 1) * 128],
                                  wsl(0, slot, hc), start=(hc == 0),
                                  stop=(hc == NC - 1)), [])
                    incs4 = []
                    if hc == NC - 1:
                        incs4 = [(gf4, 1)] if tc == NC - 1 else [(mm4, 1)]
                    step("tensor", lambda t, hc=hc, tc=tc, slot=slot, pu=pu:
                         t.matmul(pu[:], xq_fm[:, hc * S + tc * 128:
                                            hc * S + (tc + 1) * 128],
                                  wsl(1, slot, hc), start=(hc == 0),
                                  stop=(hc == NC - 1)), incs4)
                par = tc % 2
                if tc < NC - 1:
                    wait("scalar", mm4, gh * 15 + tc + 1)
                else:
                    wait("scalar", gf4, gh + 1)
                if idx >= 2:
                    wait("scalar", avr, idx - 1)
                step("scalar", lambda sc, pg=pg, par=par: sc.activation(
                    r1_sc[par][:], pg[:], ACT.Relu), [(pf4, 1)])
                wait("vector", pf4, pf4.n)
                step("vector", lambda v, pu=pu, par=par: v.tensor_mul(
                    tt_sc[par][:], r1_sc[par][:], pu[:]), [(pf4, 1)])
                if idx >= 2:
                    wait("vector", sq4, idx - 1)
                step("vector", lambda v, par=par: v.tensor_mul(
                    ar_sc[par][:], tt_sc[par][:], r1_sc[par][:]), [(avr, 1)])
                wait("scalar", avr, avr.n)
                step("scalar", lambda sc, par=par, tc=tc, gh=gh: sc.activation(
                    sq_dm[:], ar_sc[par][:], ACT.Square,
                    scale=tcol[:, 48 + tc:49 + tc],
                    accum_out=sqp[:, tc * 16 + gh:tc * 16 + gh + 1]), [(sq4, 1)])
                if idx >= 2:
                    wait("vector", sg4, 16 * (idx - 1))
                step("vector", lambda v, par=par, blk=blk, half=half:
                     v.tensor_mul(ag_st[par][:], ar_sc[par][:],
                                  g_bc[:, blk * 1024 + half * 512:
                                       blk * 1024 + half * 512 + 512]),
                     [(agv, 1)])
                wait("scalar", agv, agv.n)
                dma("scalar",
                    bass.AP(ag_spq[tc // 4].tensor,
                            (tc % 4) * 128 * II + blk * 1024 + half * 512,
                            [[II, 128], [1, 512]]), ag_st[par][:], sg4)
        barrier()

    if n_phases >= 8:
        lag = sem("lag")
        yqd = sem("yqd")
        tmy = sem("tmy")
        tvy = sem("tvy")
        wl5 = sem("wl5")
        wd5 = sem("wd5")
        mm5 = sem("mm5")
        ev5 = sem("ev5")
        lx5 = sem("lx5")
        ot5 = sem("ot5")
        gic = [0]
        for grp in range(4):
            for t4 in range(4):
                tc = grp * 4 + t4
                buf = agld[t4 % 2]
                if tc >= 2:
                    wait("sync", yqd, 4 * (tc - 1))
                if t4 == 0 and grp >= 1:
                    wait("sync", ev5, ev5.n)
                dma("sync", buf[:],
                    ag_spq[tc // 4][(tc % 4) * 128:(tc % 4 + 1) * 128, :], lag)
                wait("vector", lag)
                vstep(lambda v, b=buf: v.reduce_max(
                    stat[:, 40:41], b[:], axis=AX, apply_absolute_value=True))
                vstep(lambda v, tc=tc: v.reduce_sum(
                    stat[:, 41:42], sqp[:, tc * 16:(tc + 1) * 16], axis=AX))
                vstep(lambda v: v.tensor_single_scalar(
                    stat[:, 41:42], stat[:, 41:42], 1.0 / II, ALU.mult))
                wait("scalar", vsf)
                sstep(lambda sc: sc.activation(stat[:, 42:43], stat[:, 41:42],
                                               ACT.Sqrt, bias=stat[:, 33:34],
                                               scale=1.0))
                wait("vector", ssf)
                vstep(lambda v: v.reciprocal(stat[:, 43:44], stat[:, 42:43]))
                vstep(lambda v, tc=tc: v.tensor_mul(stat[:, 44:45],
                                                    tcol[:, 48 + tc:49 + tc],
                                                    stat[:, 43:44]))
                vstep(lambda v: v.tensor_mul(stat[:, 45:46], stat[:, 40:41],
                                             stat[:, 44:45]))
                vstep(lambda v: v.tensor_single_scalar(stat[:, 45:46],
                                                       stat[:, 45:46], QEPS,
                                                       ALU.max))
                vstep(lambda v: v.reciprocal(stat[:, 46:47], stat[:, 45:46]))
                vstep(lambda v: v.tensor_single_scalar(stat[:, 46:47],
                                                       stat[:, 46:47], 127.0,
                                                       ALU.mult))
                vstep(lambda v, tc=tc: v.reciprocal(tcol[:, 64 + tc:65 + tc],
                                                    stat[:, 46:47]))
                vstep(lambda v: v.tensor_mul(stat[:, 47:48], stat[:, 44:45],
                                             stat[:, 46:47]))
                vstep(lambda v, tc=tc: v.tensor_mul(tcol[:, 80 + tc:81 + tc],
                                                    cc(C_MWD),
                                                    tcol[:, 64 + tc:65 + tc]))
                for pc in range(4):
                    wait("scalar", vsf)
                    wait("scalar", yqd, yqd.n)
                    sstep(lambda sc, b=buf, pc=pc: sc.activation(
                        u_pc[:], b[:, pc * 2048:(pc + 1) * 2048], ACT.Identity,
                        bias=cc(C_K1), scale=stat[:, 47:48]))
                    wait("vector", ssf)
                    gpc = tc * 4 + pc
                    if gpc >= 1:
                        wait("vector", tmy, gpc * 4)
                    step("vector", lambda v: v.tensor_single_scalar(
                        yq_st[:], u_pc[:], cc(C_NEGK), ALU.add), [(yqd, 1)])
                    wait("tensor", yqd, yqd.n)
                    for k in range(NC):
                        gi = gpc * 4 + k // 4
                        bank = pbk[gi % 2]
                        if k % 4 == 0 and gi >= 2:
                            wait("tensor", tvy, gi - 1)
                        step("tensor", lambda t, k=k, bank=bank: t.matmul(
                            bank[:, (k % 4) * 128:(k % 4 + 1) * 128],
                            yq_st[:, k * 128:(k + 1) * 128], ident_bf[:],
                            start=True, stop=True),
                            [(tmy, 1)] if k % 4 == 3 else [])
                        if k % 4 == 3:
                            wait("vector", tmy)
                            yqf3 = yq_fm.ap().rearrange("p (ic t) -> p ic t",
                                                        ic=NI)
                            dst = yqf3[:, pc * 16 + k - 3:pc * 16 + k + 1,
                                       t4 * 128:(t4 + 1) * 128]
                            srcb = bank.ap().rearrange("p (j t) -> p j t", j=4)
                            step("vector", lambda v, d=dst, s=srcb:
                                 v.tensor_copy(d, s), [(tvy, 1)])
            wait("tensor", tvy, tvy.n)
            for dh in range(2):
                for ic in range(NI):
                    sl = ic % 2
                    gici = gic[0]
                    if gici >= 2:
                        wait("sync", wd5, gici - 1)
                    dmat("sync", wdT[sl][:],
                         bass.AP(wdq_sp.tensor, (dh * 1024) * II + ic * 128,
                                 [[II, 1024], [1, 128]]), wl5)
                    wait("tensor", wl5, wl5.n)
                    if ic == 0:
                        wait("tensor", ev5, ev5.n)
                    for t4 in range(4):
                        for qq in range(2):
                            bank = pbk[t4 * 2 + qq]
                            incs5 = []
                            if t4 == 3 and qq == 1:
                                incs5 = [(wd5, 1)]
                            elif ic == NI - 1 and qq == 1:
                                incs5 = [(mm5, 1)]
                            step("tensor", lambda t, sl=sl, t4=t4, qq=qq, ic=ic,
                                 bank=bank: t.matmul(
                                     bank[:],
                                     yq_fm[:, ic * 512 + t4 * 128:
                                           ic * 512 + (t4 + 1) * 128],
                                     wdT[sl][:, qq * 512:(qq + 1) * 512],
                                     start=(ic == 0), stop=(ic == NI - 1)),
                                 incs5)
                    gic[0] += 1
                for t4 in range(4):
                    tc = grp * 4 + t4
                    if t4 < 3:
                        wait("scalar", mm5, mm5.n - (2 - t4))
                    else:
                        wait("scalar", wd5, gic[0])
                    for qq in range(2):
                        bank = pbk[t4 * 2 + qq]
                        step("scalar", lambda sc, bank=bank, dh=dh, qq=qq,
                             t4=t4, tc=tc: sc.activation(
                                 dn4[:, t4 * 2048 + dh * 1024 + qq * 512:
                                     t4 * 2048 + dh * 1024 + qq * 512 + 512],
                                 bank[:], ACT.Identity,
                                 scale=tcol[:, 80 + tc:81 + tc]), [(ev5, 1)])
            for t4 in range(4):
                tc = grp * 4 + t4
                wait("sync", ot5, ot5.n)
                dma("sync", x1ld[:], x1_sp[tc * 128:(tc + 1) * 128, :], lx5)
                wait("vector", ev5, ev5.n)
                wait("vector", lx5, lx5.n)
                vstep(lambda v, t4=t4: v.tensor_add(
                    dn_sb[:], dn4[:, t4 * 2048:(t4 + 1) * 2048], x1ld[:]))
                wait("scalar", vsf)
                sstep(lambda sc: sc.activation(x1ld[:], dn_sb[:], ACT.Identity,
                                               accum_out=stat[:, 52:53]))
                sstep(lambda sc: sc.activation(x1ld[:], dn_sb[:], ACT.Square,
                                               accum_out=stat[:, 53:54]))
                wait("vector", ssf)
                vstep(lambda v: v.tensor_single_scalar(
                    stat[:, 52:53], stat[:, 52:53], 1.0 / D, ALU.mult))
                vstep(lambda v: v.tensor_single_scalar(
                    stat[:, 53:54], stat[:, 53:54], 1.0 / D, ALU.mult))
                vstep(lambda v: v.tensor_mul(stat[:, 54:55], stat[:, 52:53],
                                             stat[:, 52:53]))
                vstep(lambda v: v.tensor_sub(stat[:, 54:55], stat[:, 53:54],
                                             stat[:, 54:55]))
                wait("scalar", vsf)
                sstep(lambda sc: sc.activation(stat[:, 55:56], stat[:, 54:55],
                                               ACT.Sqrt, bias=stat[:, 32:33],
                                               scale=1.0))
                wait("vector", ssf)
                vstep(lambda v: v.reciprocal(stat[:, 56:57], stat[:, 55:56]))
                vstep(lambda v: v.scalar_tensor_tensor(
                    stat[:, 57:58], stat[:, 52:53], -1.0, stat[:, 56:57],
                    ALU.mult, ALU.mult))
                wait("scalar", vsf)
                sstep(lambda sc: sc.activation(
                    x1ld[:], dn_sb[:], ACT.Identity, bias=stat[:, 57:58],
                    scale=stat[:, 56:57]), [(ev5, 1)])
                dma("scalar", out[tc * 128:(tc + 1) * 128, :], x1ld[:], ot5)
        wait("tensor", ot5, ot5.n)
        wait("sync", ot5, ot5.n)

    if debug:
        sdbg = sem("sdbg")
        srcs = {"qkv_sp": qkv_sp, "x1_sp": x1_sp,
                "wgq_sp": wgq_sp, "wuq_sp": wuq_sp, "wdq_sp": wdq_sp}
        for name in debug:
            if name == "ag_sp":
                for gq in range(4):
                    dma("scalar", dbg_outs[name][gq * 512:(gq + 1) * 512, :],
                        ag_spq[gq][:, :], sdbg)
                continue
            if name == "consts":
                dma("scalar", dbg_outs[name], c_pool[:], sdbg)
            elif name == "x_fm":
                dma("scalar", dbg_outs[name], x_fm[:], sdbg)
            elif name == "tcol":
                dma("scalar", dbg_outs[name], tcol[:], sdbg)
            elif name == "dn4":
                dma("scalar", dbg_outs[name], dn4[:], sdbg)
            else:
                big = srcs[name]
                nr = big.shape[0]
                chunk = 512
                for r0 in range(0, nr, chunk):
                    dma("scalar", dbg_outs[name][r0:r0 + chunk, :],
                        big[r0:r0 + chunk, :], sdbg)
        wait("scalar", sdbg, sdbg.n)

    with stack:
        with nc.Block() as block:
            for e in ENGINES:
                def runner(eng, e=e):
                    for fn in steps[e]:
                        fn(eng)
                getattr(block, e)(runner)
    return nc


def kernel(L0_attn_x, L0_attn_Wqkv, L0_attn_bqkv, L0_attn_Wo, L0_attn_bo,
           L0_ffn_w_gate, L0_ffn_w_up, L0_ffn_w_down, L0_ffn_subln_weight):
    nc = build_program()
    x = np.ascontiguousarray(np.asarray(L0_attn_x, dtype=np.float32))
    shared = dict(
        wqkv=np.ascontiguousarray(np.asarray(L0_attn_Wqkv, np.float32)),
        bqkv=np.ascontiguousarray(np.asarray(L0_attn_bqkv, np.float32)),
        wo=np.ascontiguousarray(np.asarray(L0_attn_Wo, np.float32)),
        bo=np.ascontiguousarray(np.asarray(L0_attn_bo, np.float32)),
        wg=np.ascontiguousarray(np.asarray(L0_ffn_w_gate, np.float32)),
        wu=np.ascontiguousarray(np.asarray(L0_ffn_w_up, np.float32)),
        wd=np.ascontiguousarray(np.asarray(L0_ffn_w_down, np.float32)),
        gsub=np.ascontiguousarray(np.asarray(L0_ffn_subln_weight, np.float32)),
    )
    in_maps = [dict(x=np.ascontiguousarray(x[b]), **shared) for b in range(8)]
    res = run_bass_kernel_spmd(nc, in_maps, core_ids=list(range(8)))
    return np.stack([res.results[b]["out"] for b in range(8)], axis=0)
